# version 59
# baseline (speedup 1.0000x reference)
import numpy as np
import ml_dtypes
from contextlib import ExitStack
import concourse.bass as bass
import concourse.mybir as mybir
from concourse.bass_utils import run_bass_kernel_spmd

F32 = mybir.dt.float32
BF16 = mybir.dt.bfloat16
AF = mybir.ActivationFunctionType
ALU = mybir.AluOpType
AX = mybir.AxisListType

C = 1024
H = 4096
CCH = C // 128
HCH = H // 128
WINDOW = 16
NEG_INF = -1e30
EPS = 1e-5

ENGS = ("tensor", "vector", "scalar", "gpsimd", "sync")


class Sched:

    NSEM = 8

    def __init__(self):
        self.items = {e: [] for e in ENGS}
        self.planned = {}
        self.rr = {}

    def emit(self, eng, fn, after=(), inc=None, amt=1):
        co = {}
        for e in after:
            if e is None:
                continue
            s, v = e
            if v > 0 and v > co.get(s, 0):
                co[s] = v
        ev = None
        if inc is not None:
            self.planned[inc] = self.planned.get(inc, 0) + amt
            ev = (inc, self.planned[inc])
        self.items[eng].append((tuple(co.items()), fn, (inc, amt) if inc else None))
        return ev

    def dma(self, queue_eng, out, in_, after=(), sem="LD"):
        if sem in ("LD", "GD", "OS", "SD"):
            n = self.rr.get(sem, 0)
            self.rr[sem] = n + 1
            sem = f"{sem}{n % self.NSEM}"

        def fn(eng, out=out, in_=in_):
            return eng.dma_start(out=out, in_=in_)
        return self.emit(queue_eng, fn, after=after, inc=sem, amt=16)

    def snap(self):
        return [(s, self.planned.get(s, 0)) for s in ("TC", "VC", "SC")]

    def snap_dma(self):
        out = []
        for base in ("LD", "GD", "OS", "SD"):
            for i in range(self.NSEM):
                nm = f"{base}{i}"
                v = self.planned.get(nm, 0)
                if v:
                    out.append((nm, v))
        return out

    def check_deadlock(self):
        pos = {e: 0 for e in ENGS}
        vals = {}
        progress = True
        while progress:
            progress = False
            for e in ENGS:
                while pos[e] < len(self.items[e]):
                    after, _, inc = self.items[e][pos[e]]
                    if any(vals.get(s, 0) < v for (s, v) in after):
                        break
                    if inc:
                        s, a = inc
                        vals[s] = vals.get(s, 0) + a
                    pos[e] += 1
                    progress = True
        stuck = {e: (pos[e], len(self.items[e])) for e in ENGS
                 if pos[e] < len(self.items[e])}
        if stuck:
            det = {}
            for e in stuck:
                after, _, _ = self.items[e][pos[e]]
                det[e] = [(s, v, "planned", self.planned.get(s, 0),
                           "reached", vals.get(s, 0)) for (s, v) in after]
            raise RuntimeError(f"schedule deadlock: stuck={stuck} waits={det}")

    def replay(self, block, sems):
        waited = {e: {} for e in ENGS}

        def mk(ename):
            def body(eng):
                w = waited[ename]
                for after, fn, inc in self.items[ename]:
                    for (s, v) in after:
                        if w.get(s, 0) < v:
                            eng.wait_ge(sems[s], v)
                            w[s] = v
                    ins = fn(eng)
                    if inc is not None:
                        s, a = inc
                        ins.then_inc(sems[s], a)
            return body
        block.tensor(mk("tensor"))
        block.vector(mk("vector"))
        block.scalar(mk("scalar"))
        block.gpsimd(mk("gpsimd"))
        block.sync(mk("sync"))


def ceil_div(a, b):
    return (a + b - 1) // b


def build_program(T, debug=False):
    NT = T // 128
    NSP = T // 512
    nc = bass.Bass("TRN2", target_bir_lowering=False, debug=False)

    x_h = nc.dram_tensor("x", [T, C], F32, kind="ExternalInput").ap()
    wqT_h = nc.dram_tensor("wqT", [C, C], F32, kind="ExternalInput").ap()
    wkT_h = nc.dram_tensor("wkT", [C, C], F32, kind="ExternalInput").ap()
    wvT_h = nc.dram_tensor("wvT", [C, C], F32, kind="ExternalInput").ap()
    woT_h = nc.dram_tensor("woT", [C, C], F32, kind="ExternalInput").ap()
    w1T_h = nc.dram_tensor("w1T", [C, H], F32, kind="ExternalInput").ap()
    w2T_h = nc.dram_tensor("w2T", [H, C], F32, kind="ExternalInput").ap()
    bqp_h = nc.dram_tensor("bqp", [128, CCH], F32, kind="ExternalInput").ap()
    bkp_h = nc.dram_tensor("bkp", [128, CCH], F32, kind="ExternalInput").ap()
    b1p_h = nc.dram_tensor("b1p", [128, HCH], F32, kind="ExternalInput").ap()
    bvp_h = nc.dram_tensor("bvp", [128, CCH], F32, kind="ExternalInput").ap()
    bor_h = nc.dram_tensor("bor", [1, C], F32, kind="ExternalInput").ap()
    b2r_h = nc.dram_tensor("b2r", [1, C], F32, kind="ExternalInput").ap()
    iden_h = nc.dram_tensor("iden", [128, 128], BF16, kind="ExternalInput").ap()
    mask_h = nc.dram_tensor("maskst", [128, 256], F32, kind="ExternalInput").ap()
    out_h = nc.dram_tensor("out", [T, C], F32, kind="ExternalOutput").ap()

    KIND = "ExternalOutput" if debug else "Internal"
    qT_s = nc.dram_tensor("qT_s", [T // 128, 128, C], BF16, kind=KIND).ap()
    y_s = nc.dram_tensor("y_s", [NT, 128, C], BF16, kind=KIND).ap()
    xn_s = nc.dram_tensor("xn_s", [T, C], F32, kind=KIND).ap()
    bws1_s = nc.dram_tensor("bws1_s", [CCH, 128, H], BF16, kind=KIND).ap()
    bws2_s = nc.dram_tensor("bws2_s", [HCH, 128, C], BF16, kind=KIND).ap()
    woT_bf = nc.dram_tensor("woT_bf", [C, C], BF16, kind=KIND).ap()
    if debug:
        dbg_b2sb = nc.dram_tensor("dbg_b2sb", [128, HCH, C], F32, kind=KIND).ap()
        dbg_a2p4 = nc.dram_tensor("dbg_a2p4", [128, C], F32, kind=KIND).ap()
        dbg_hT = nc.dram_tensor("dbg_hT", [128, HCH, 512], F32, kind=KIND).ap()
        dbg_a1 = nc.dram_tensor("dbg_a1", [128, HCH], F32, kind=KIND).ap()
        dbg_a2 = nc.dram_tensor("dbg_a2", [128, C], F32, kind=KIND).ap()
        dbg_kT = nc.dram_tensor("dbg_kT", [128, CCH, T], F32, kind=KIND).ap()
        dbg_v = nc.dram_tensor("dbg_v", [NT, 128, C], F32, kind=KIND).ap()

    S = Sched()

    stp = ExitStack()
    A = lambda st, *a: st.enter_context(nc.sbuf_tensor(*a))
    P = lambda st, *a: st.enter_context(nc.psum_tensor(*a))
    if True:
        iden_sb = A(stp, "iden_sb", [128, 128], BF16)
        onesr = A(stp, "onesr", [1, 128], BF16)
        onesc = A(stp, "onesc", [128, 1], BF16)
        alpha1_pp = A(stp, "alpha1_pp", [128, HCH], F32)
        alpha2_pp = A(stp, "alpha2_pp", [128, CCH], F32)
        e_iden = S.dma("sync", iden_sb[:], iden_h[:, :])
        e_ones = S.emit("gpsimd", lambda g: g.memset(onesr[:, :], 1.0), inc="GC")
        e_onesc = S.emit("gpsimd", lambda g: g.memset(onesc[:, :], 1.0), inc="GC")

        stkv = ExitStack()
        if True:
            kT_sb = A(stkv, "kT_sb", [128, CCH, T], BF16)
            v_sb = A(stkv, "v_sb", [128, NT, C], BF16)
            st1 = ExitStack()
            if True:
                wq_sb = A(st1, "wq_sb", [128, CCH, C], BF16)
                wk_sb = A(st1, "wk_sb", [128, CCH, C], BF16)
                wv_sb = A(st1, "wv_sb", [128, CCH, C], BF16)
                x_in = A(st1, "x_in", [128, 3, C], F32)
                ln_rg = A(st1, "ln_rg", [128, 4, C], BF16)
                lnT_sb = A(st1, "lnT_sb", [128, CCH, 512], BF16)
                qsp = A(st1, "qsp", [128, 1, 512], BF16)
                bqp_sb = A(st1, "bqp_sb", [128, CCH], F32)
                bkp_sb = A(st1, "bkp_sb", [128, CCH], F32)
                st_sum = A(st1, "st_sum", [128, 4], F32)
                st_m2 = A(st1, "st_m2", [128, 4], F32)
                st_ssq = A(st1, "st_ssq", [128, 4], F32)
                st_ngm = A(st1, "st_ngm", [128, 4], F32)
                st_m2e = A(st1, "st_m2e", [128, 4], F32)
                st_rsd = A(st1, "st_rsd", [128, 4], F32)
                p1_ka = P(st1, "p1_ka", [128, 512], F32)
                p1_kb = P(st1, "p1_kb", [128, 512], F32)
                p1_qa = P(st1, "p1_qa", [128, 512], F32)
                p1_qb = P(st1, "p1_qb", [128, 512], F32)
                p1_va = P(st1, "p1_va", [128, 512], F32)
                p1_vb = P(st1, "p1_vb", [128, 512], F32)
                p1_ta = P(st1, "p1_ta", [128, 4, 128], BF16)
                p1_tb = P(st1, "p1_tb", [128, 4, 128], BF16)
                e_bqp = S.dma("sync", bqp_sb[:], bqp_h[:, :])
                e_bkp = S.dma("sync", bkp_sb[:], bkp_h[:, :])
                e_wq, e_wk, e_wv = [], [], []
                for (wsb, wh, evl) in ((wk_sb, wkT_h, e_wk), (wq_sb, wqT_h, e_wq),
                                       (wv_sb, wvT_h, e_wv)):
                    for cc in range(CCH):
                        evl.append(S.dma("gpsimd", wsb[:, cc, :],
                                         wh[cc * 128:(cc + 1) * 128, :], sem="GD"))

                xld = [None] * NT
                lnv = [None] * NT
                sqe = [None] * NT
                ltev = [None] * NT
                kev_all = []
                qspill = {}

                kbank = [p1_ka, p1_kb]
                qbank = [p1_qa, p1_qb]
                vbank = [p1_va, p1_vb]
                tbank = [p1_ta, p1_tb]
                kb_ev = [None, None]
                qb_ev = [None, None]
                vb_ev = [None, None]
                tb_ev = [None, None]
                qsp_ev = [None, None]

                def ln_tile_sum(t, x_ap, ld_ev, stats):
                    (ssum, sssq, sngm, sm2, sm2e, srsd) = stats
                    par = t % 4
                    return S.emit("vector", lambda v, par=par: v.tensor_reduce(
                        ssum[:, par:par + 1], x_ap, AX.X, ALU.add),
                        after=[ld_ev], inc="VC")

                def ln_tile_sq(t, x_ap, ld_ev, ring, stats, lnconsumed,
                               sqe_arr):
                    (ssum, sssq, sngm, sm2, sm2e, srsd) = stats
                    par = t % 4
                    dep_sq = [ld_ev]
                    if lnconsumed is not None:
                        dep_sq.append(lnconsumed)
                    sqe_arr[t] = S.emit("scalar", lambda s, par=par: s.activation(
                        ring[:, par, :], x_ap, AF.Square, bias=0.0, scale=1.0,
                        accum_out=sssq[:, par:par + 1]),
                        after=dep_sq, inc="SC")
                    return sqe_arr[t]

                def ln_tile_a(t, x_ap, ld_ev, ring, stats, lnconsumed,
                              sqe_arr):
                    e_sum = ln_tile_sum(t, x_ap, ld_ev, stats)
                    ln_tile_sq(t, x_ap, ld_ev, ring, stats, lnconsumed, sqe_arr)
                    return e_sum

                def ln_tile_b1(t, stats, e_sum, sqe_arr):
                    (ssum, sssq, sngm, sm2, sm2e, srsd) = stats
                    par = t % 4
                    e_ngm = S.emit("vector", lambda v, par=par: v.tensor_scalar(
                        sngm[:, par:par + 1], ssum[:, par:par + 1],
                        -1.0 / C, None, ALU.mult),
                        after=[e_sum], inc="VC")
                    e_m2 = S.emit("vector", lambda v, par=par: v.scalar_tensor_tensor(
                        sm2[:, par:par + 1], sngm[:, par:par + 1], 1.0,
                        sngm[:, par:par + 1], ALU.mult, ALU.mult),
                        after=[e_ngm], inc="VC")
                    e_m2e = S.emit("vector", lambda v, par=par: v.tensor_scalar(
                        sm2e[:, par:par + 1], sm2[:, par:par + 1], -1.0, EPS,
                        ALU.mult, ALU.add),
                        after=[e_m2], inc="VC")
                    e_vep = S.emit("vector", lambda v, par=par: v.tensor_scalar(
                        sm2[:, par:par + 1], sssq[:, par:par + 1], 1.0 / C,
                        sm2e[:, par:par + 1], ALU.mult, ALU.add),
                        after=[sqe_arr[t], e_m2e], inc="VC")
                    return e_ngm, e_vep

                def ln_tile_bstd(t, stats, e_vep):
                    (ssum, sssq, sngm, sm2, sm2e, srsd) = stats
                    par = t % 4
                    return S.emit("scalar", lambda s, par=par: s.activation(
                        sm2e[:, par:par + 1], sm2[:, par:par + 1], AF.Sqrt,
                        bias=0.0, scale=1.0),
                        after=[e_vep], inc="SC")

                def ln_tile_b2(t, x_ap, ld_ev, ring, stats, e_ngm, e_std,
                               lnv_arr, sqe_arr, ring_guard=None):
                    (ssum, sssq, sngm, sm2, sm2e, srsd) = stats
                    par = t % 4
                    e_rsd = S.emit("vector", lambda v, par=par: v.reciprocal(
                        srsd[:, par:par + 1], sm2e[:, par:par + 1]),
                        after=[e_std], inc="VC")
                    dep = [e_ngm, e_rsd, ld_ev, sqe_arr[t], ring_guard]
                    lnv_arr[t] = S.emit("vector", lambda v, par=par: v.tensor_scalar(
                        ring[:, par, :], x_ap,
                        sngm[:, par:par + 1], srsd[:, par:par + 1],
                        ALU.add, ALU.mult),
                        after=dep, inc="VC")

                def transpose_8(src_fn, src_ev, dest_sb, dest_off, banks, b_ev,
                                extra=(), evac_eng="vector"):
                    evacs = []
                    lastpe = None
                    inc_sem = "VC" if evac_eng == "vector" else "SC"
                    for g in range(2):
                        bank = banks[g]
                        dep = [src_ev, e_iden, b_ev[g]] + list(extra)
                        last = None
                        for k in range(4):
                            ck = g * 4 + k
                            last = S.emit("tensor",
                                          lambda te, bank=bank, k=k, ck=ck: te.transpose(
                                              bank[:, k, :], src_fn(ck), iden_sb[:]),
                                          after=dep if k == 0 else (), inc="TC")
                        if evac_eng == "vector":
                            ev = S.emit("vector",
                                        lambda v, bank=bank, g=g, dest_off=dest_off,
                                        dest_sb=dest_sb: v.tensor_copy(
                                            dest_sb[:, g * 4:(g + 1) * 4,
                                                    dest_off:dest_off + 128],
                                            bank[:, 0:4, :]),
                                        after=[last], inc=inc_sem)
                        else:
                            ev = S.emit("scalar",
                                        lambda s, bank=bank, g=g, dest_off=dest_off,
                                        dest_sb=dest_sb: s.activation(
                                            dest_sb[:, g * 4:(g + 1) * 4,
                                                    dest_off:dest_off + 128],
                                            bank[:, 0:4, :], AF.Copy,
                                            bias=0.0, scale=1.0),
                                        after=[last], inc=inc_sem)
                        b_ev[g] = ev
                        evacs.append(ev)
                        lastpe = last
                    return lastpe, evacs

                stats1 = (st_sum, st_ssq, st_ngm, st_m2, st_m2e, st_rsd)

                sums = {}
                b1ev = {}

                def _xld_sum(t, dep):
                    xld[t] = S.dma("sync", x_in[:, t % 3, :],
                                   x_h[t * 128:(t + 1) * 128, :], after=dep)
                    sums[t] = ln_tile_sum(t, x_in[:, t % 3, :], xld[t], stats1)

                def _sq(t):
                    g1 = t // 4
                    sqe[t] = S.emit("scalar", lambda s, t=t, g1=g1: s.activation(
                        kT_sb[:, 0:2, g1 * 512:g1 * 512 + 512],
                        x_in[:, t % 3, :], AF.Square, bias=0.0, scale=1.0,
                        accum_out=st_ssq[:, (t % 4):(t % 4) + 1]),
                        after=[xld[t]], inc="SC")

                def _b2(t):
                    ln_tile_b2(t, x_in[:, t % 3, :], xld[t], ln_rg, stats1,
                               b1ev[t][0], stds1[t], lnv, sqe,
                               ltev[t - 4] if t >= 4 else None)

                stds1 = {}

                def chains_A(sp):
                    for t in range(sp * 4, sp * 4 + 3):
                        dep = []
                        if t >= 3:
                            dep = [lnv[t - 3], sqe[t - 3]]
                        _xld_sum(t, dep)

                def chain_fin(sp, j):
                    t0 = sp * 4
                    t3 = t0 + 3
                    if j == 0:
                        for t in range(t0, t0 + 3):
                            _sq(t)
                    elif j == 1:
                        for t in range(t0, t0 + 3):
                            b1ev[t] = ln_tile_b1(t, stats1, sums[t], sqe)
                        stds1[t0] = ln_tile_bstd(t0, stats1, b1ev[t0][1])
                        _b2(t0)
                        _xld_sum(t3, [lnv[t3 - 3], sqe[t3 - 3]])
                        _sq(t3)
                    elif j == 2:
                        stds1[t0 + 1] = ln_tile_bstd(t0 + 1, stats1,
                                                     b1ev[t0 + 1][1])
                        _b2(t0 + 1)
                        stds1[t0 + 2] = ln_tile_bstd(t0 + 2, stats1,
                                                     b1ev[t0 + 2][1])
                        _b2(t0 + 2)
                        b1ev[t3] = ln_tile_b1(t3, stats1, sums[t3], sqe)
                    else:
                        stds1[t3] = ln_tile_bstd(t3, stats1, b1ev[t3][1])
                        _b2(t3)

                chains_A(0)
                for j in range(4):
                    chain_fin(0, j)
                for sp in range(NSP):
                    evac_evs = []
                    for ttloc in range(4):
                        t = sp * 4 + ttloc
                        lt, tevs = transpose_8(
                            (lambda ck, t=t: ln_rg[:, t % 4, ck * 128:(ck + 1) * 128]),
                            lnv[t], lnT_sb, ttloc * 128, tbank, tb_ev)
                        ltev[t] = lt
                        evac_evs += tevs
                    if sp + 1 < NSP:
                        chains_A(sp + 1)
                    for cc in range(CCH):
                        for which in ("k", "q"):
                            bank = (kbank if which == "k" else qbank)[cc % 2]
                            bev = kb_ev if which == "k" else qb_ev
                            wsb = wk_sb if which == "k" else wq_sb
                            wld = e_wk if which == "k" else e_wq
                            dep = list(evac_evs) + list(wld) + [bev[cc % 2]]
                            last = None
                            for c in range(CCH):
                                last = S.emit("tensor",
                                              lambda te, bank=bank, c=c, cc=cc, wsb=wsb: te.matmul(
                                                  bank[:, :],
                                                  wsb[:, c, cc * 128:(cc + 1) * 128],
                                                  lnT_sb[:, c, :],
                                                  start=(c == 0), stop=(c == CCH - 1)),
                                              after=dep if c == 0 else (),
                                              inc=None if c < CCH - 1 else "TC")
                            if which == "k":
                                ev = S.emit("scalar",
                                            lambda s, bank=bank, cc=cc, sp=sp: s.activation(
                                                kT_sb[:, cc, sp * 512:(sp + 1) * 512],
                                                bank[:, :], AF.Identity,
                                                bias=bkp_sb[:, cc:cc + 1], scale=1.0),
                                            after=[last, e_bkp], inc="SC")
                                kb_ev[cc % 2] = ev
                                kev_all.append(ev)
                            else:
                                qslot = 0
                                ev = S.emit("scalar",
                                            lambda s, bank=bank, cc=cc, qslot=qslot: s.activation(
                                                qsp[:, qslot, :], bank[:, :], AF.Identity,
                                                bias=bqp_sb[:, cc:cc + 1], scale=1.0),
                                            after=[last, e_bqp, qsp_ev[qslot]],
                                            inc="SC")
                                qb_ev[cc % 2] = ev
                                qspill[(cc, sp)] = S.dma(
                                    "gpsimd",
                                    qT_s[sp * 4:(sp + 1) * 4, :,
                                         cc * 128:(cc + 1) * 128]
                                    .rearrange("a p b -> p a b"),
                                    qsp[:, qslot, :].rearrange(
                                        "p (a b) -> p a b", a=4),
                                    after=[ev], sem="GD")
                                qsp_ev[qslot] = qspill[(cc, sp)]
                    vring = [vbank[0], vbank[1], kbank[0], kbank[1]]
                    vre = [vb_ev[0], vb_ev[1], kb_ev[0], kb_ev[1]]
                    for ttloc in range(4):
                        t = sp * 4 + ttloc
                        for h in range(2):
                            bidx = (ttloc * 2 + h) % 4
                            bank = vring[bidx]
                            dep = list(evac_evs) + list(e_wv) + [vre[bidx]]
                            last = None
                            for c in range(CCH):
                                last = S.emit("tensor",
                                              lambda te, bank=bank, c=c, ttloc=ttloc, h=h: te.matmul(
                                                  bank[:, :],
                                                  lnT_sb[:, c, ttloc * 128:(ttloc + 1) * 128],
                                                  wv_sb[:, c, h * 512:(h + 1) * 512],
                                                  start=(c == 0), stop=(c == CCH - 1)),
                                              after=dep if c == 0 else (),
                                              inc=None if c < CCH - 1 else "TC")
                            ev = S.emit("scalar",
                                        lambda s, bank=bank, t=t, h=h: s.activation(
                                            v_sb[:, t, h * 512:(h + 1) * 512],
                                            bank[:, :], AF.Copy, bias=0.0,
                                            scale=1.0),
                                        after=[last], inc="SC")
                            vre[bidx] = ev
                        if sp + 1 < NSP:
                            chain_fin(sp + 1, ttloc)
                    vb_ev[0], vb_ev[1] = vre[0], vre[1]
                    kb_ev[0], kb_ev[1] = vre[2], vre[3]

                ph1_snap_c = S.snap()
                ph1_snap = ph1_snap_c + S.snap_dma()
                if debug:
                    S.dma("gpsimd", dbg_kT[:, :, :], kT_sb[:, :, :],
                          after=ph1_snap, sem="GD")
                    for t in range(NT):
                        S.dma("gpsimd", dbg_v[t, :, :], v_sb[:, t, :],
                              after=ph1_snap, sem="GD")
            st1.close()

            SW = NT * 128
            st2 = ExitStack()
            if True:
                mask_sb = A(st2, "mask_sb", [128, 256], BF16)
                s_sb = A(st2, "s_sb", [128, 2, SW], F32)
                att_sb = A(st2, "att_sb", [128, SW], BF16)
                TS = NT - 8 if NT >= 24 else NT
                att2_sb = A(st2, "att2_sb", [128, (NT - TS) * 128], BF16) \
                    if TS < NT else None
                attT_sb = A(st2, "attT_sb", [128, NT, 128], BF16)
                q_in = A(st2, "q_in", [128, 3, C], BF16)
                y_sb = A(st2, "y_sb", [128, 2, C], BF16)
                gmax = A(st2, "gmax", [128, 2, 8], F32)
                ngmx = A(st2, "ngmx", [128, 2], F32)
                lsum = A(st2, "lsum", [128, 2], F32)
                rinv = A(st2, "rinv", [128, 2], F32)
                w2chunk = A(st2, "w2chunk", [128, 2, C], F32)
                w2abs = A(st2, "w2abs", [128, 4, C], BF16)
                w2sgn = A(st2, "w2sgn", [128, 1, C], BF16)
                p2_sa = P(st2, "p2_sa", [128, 512], F32)
                p2_sb = P(st2, "p2_sb", [128, 512], F32)
                p2_ta = P(st2, "p2_ta", [128, 4, 128], BF16)
                p2_tb = P(st2, "p2_tb", [128, 4, 128], BF16)
                p2_y0 = P(st2, "p2_y0", [128, 512], F32)
                p2_y1 = P(st2, "p2_y1", [128, 512], F32)
                p2_al = P(st2, "p2_al", [128, 512], F32)
                p2_sc = P(st2, "p2_sc", [128, 512], F32)
                sbank = [p2_sa, p2_sb, p2_sc]
                tbank2 = [p2_ta, p2_tb]
                sb_ev = [None, None, None]
                tb2_ev = [None, None]
                y_ev = [None, None]
                ysb_st = [None, None]
                qld = [None] * NT
                exp_ev = [None] * NT
                ltr_ev = [None] * NT
                att_last = {0: None, 1: None}
                ymm_done = [None] * NT
                smm_last = [None] * NT
                sgrp_evac = [None] * NT
                maskadd = [None] * NT
                ngx_ev = [None] * NT
                rinv_ev = [None] * NT
                ys_spill = [None] * NT
                e_wocast = S.dma("gpsimd", woT_bf[:, :], woT_h[:, :],
                                 sem="GD")
                e_mask = S.dma("gpsimd", mask_sb[:], mask_h[:, :],
                               after=ph1_snap, sem="GD")
                gidx = [0]
                tgidx = [0]
                first2 = {"pe": ph1_snap_c, "v": ph1_snap, "s": ph1_snap,
                          "sync": ph1_snap, "gp": ph1_snap}

                def gate(key):
                    return first2[key]

                NW2 = HCH
                w2_ld = [None] * NW2
                w2_abs_ev = [None] * NW2
                w2_sgn_ev = [None] * NW2
                w2_spill = [None] * NW2
                w2_mm = [None] * NW2

                def w2_chain(i):
                    sl = i % 2
                    sa = i % 4
                    dep = gate("sync")
                    if i >= 2:
                        dep += [w2_abs_ev[i - 2], w2_sgn_ev[i - 2]]
                    w2_ld[i] = S.dma("sync", w2chunk[:, sl, :],
                                     w2T_h[i * 128:(i + 1) * 128, :], after=dep)
                    d2 = [w2_ld[i]]
                    if i >= 4:
                        assert w2_mm[i - 4] is not None
                        d2.append(w2_mm[i - 4])
                    w2_abs_ev[i] = S.emit("scalar", lambda s, sl=sl, sa=sa: s.activation(
                        w2abs[:, sa, :], w2chunk[:, sl, :], AF.Abs,
                        bias=0.0, scale=1.0), after=d2, inc="SC")
                    d3 = [w2_ld[i]]
                    if i >= 1:
                        d3.append(w2_spill[i - 1])
                    w2_sgn_ev[i] = S.emit("scalar", lambda s, sl=sl: s.sign(
                        w2sgn[:, 0, :], w2chunk[:, sl, :]), after=d3, inc="SC")
                    w2_spill[i] = S.dma("sync", bws2_s[i, :, :],
                                        w2sgn[:, 0, :], after=[w2_sgn_ev[i]])

                w2_next_mm = [0]

                def w2_mms_through(k):
                    while w2_next_mm[0] <= min(k, NW2 - 1):
                        w2_mmpart(w2_next_mm[0])
                        w2_next_mm[0] += 1

                def w2_mmpart(i):
                    sa = i % 4
                    last = None
                    dep4 = [w2_abs_ev[i], e_onesc] + (gate("pe") if i == 0 else [])
                    for cp in range(CCH):
                        last = S.emit("tensor",
                                      lambda te, cp=cp, i=i, sa=sa: te.matmul(
                                          p2_al[:, cp:cp + 1],
                                          w2abs[:, sa, cp * 128:(cp + 1) * 128],
                                          onesc[:, 0:1],
                                          start=(i == 0 and cp == 0),
                                          stop=(i == NW2 - 1 and cp == CCH - 1)),
                                      after=dep4 if cp == 0 else (),
                                      inc=None if cp < CCH - 1 else "TC")
                    w2_mm[i] = last

                def qprefetch(it):
                    dep = [qspill[(cc, it // 4)] for cc in range(CCH)]
                    dep += gate("gp")
                    if it >= 3:
                        dep.append(smm_last[it - 3])
                    qld[it] = S.dma("gpsimd", q_in[:, it % 3, :],
                                    qT_s[it, :, :], after=dep, sem="GD")

                def do_s(it):
                    par = it % 2
                    jlo = max(it - 1, 0)
                    M = NT - jlo
                    ngr = ceil_div(M, 4)
                    grp = []
                    for g in range(ngr):
                        c0 = g * 4
                        w = min(4, M - c0) * 128
                        bank = sbank[gidx[0] % 3]
                        dep = [qld[it], sb_ev[gidx[0] % 3]] + kev_all
                        if it == 0 and g == 0:
                            dep += gate("pe")
                        last = None
                        for c in range(CCH):
                            last = S.emit("tensor",
                                          lambda te, bank=bank, c=c, it=it, jlo=jlo, c0=c0, w=w: te.matmul(
                                              bank[:, 0:w],
                                              q_in[:, it % 3, c * 128:(c + 1) * 128],
                                              kT_sb[:, c, (jlo + c0) * 128:(jlo + c0) * 128 + w],
                                              start=(c == 0), stop=(c == CCH - 1)),
                                          after=dep if c == 0 else (),
                                          inc=None if c < CCH - 1 else "TC")
                        smm_last[it] = last
                        dep2 = [last] + gate("v")
                        if it >= 2:
                            dep2.append(exp_ev[it - 2])
                        ev = S.emit("vector",
                                    lambda v, bank=bank, par=par, c0=c0, w=w, g=g: v.tensor_scalar(
                                        s_sb[:, par, c0 * 128:c0 * 128 + w],
                                        bank[:, 0:w], 0.03125, None,
                                        ALU.mult, ALU.max,
                                        accum_out=gmax[:, par, g:g + 1]),
                                    after=dep2, inc="VC")
                        sb_ev[gidx[0] % 3] = ev
                        grp.append(ev)
                        gidx[0] += 1
                    sgrp_evac[it] = grp
                    if it == 0:
                        maskadd[it] = S.emit("vector", lambda v: v.scalar_tensor_tensor(
                            s_sb[:, 0, 0:128], s_sb[:, 0, 0:128], 1.0,
                            mask_sb[:, 128:256], ALU.mult, ALU.add),
                            after=[grp[0], e_mask], inc="VC")
                    else:
                        maskadd[it] = S.emit("vector",
                                             lambda v, par=par: v.scalar_tensor_tensor(
                                                 s_sb[:, par, 0:256],
                                                 s_sb[:, par, 0:256], 1.0,
                                                 mask_sb[:, 0:256], ALU.mult, ALU.add),
                                             after=[grp[0], e_mask], inc="VC")
                    ngx_ev[it] = S.emit("vector",
                                        lambda v, par=par, ngr=ngr: v.tensor_reduce(
                                            ngmx[:, par:par + 1],
                                            gmax[:, par, 0:ngr], AX.X, ALU.max,
                                            negate=True),
                                        after=grp, inc="VC")

                def att_of(it):
                    if it >= TS and it % 2 == 1:
                        return att2_sb, 1
                    return att_sb, 0

                def do_exp(it):
                    par = it % 2
                    jlo = max(it - 1, 0)
                    L = (NT - jlo) * 128
                    abuf, akey = att_of(it)
                    dep = list(sgrp_evac[it]) + [maskadd[it], ngx_ev[it]] + gate("s")
                    dep.append(att_last[akey])
                    exp_ev[it] = S.emit("scalar",
                                        lambda s, par=par, L=L, abuf=abuf: s.activation(
                                            abuf[:, 0:L], s_sb[:, par, 0:L],
                                            AF.Exp, bias=ngmx[:, par:par + 1],
                                            scale=1.0,
                                            accum_out=lsum[:, par:par + 1]),
                                        after=dep, inc="SC")
                    rinv_ev[it] = S.emit("vector", lambda v, par=par: v.reciprocal(
                        rinv[:, par:par + 1], lsum[:, par:par + 1]),
                        after=[exp_ev[it]], inc="VC")

                def do_ty(it):
                    par = it % 2
                    jlo = max(it - 1, 0)
                    M = NT - jlo
                    ngr4 = ceil_div(M, 4)
                    abuf, akey = att_of(it)
                    att_evac = [None] * ngr4
                    for g in range(ngr4):
                        n_in_g = min(4, M - g * 4)
                        bank = tbank2[tgidx[0] % 2]
                        dep = [exp_ev[it], e_iden, tb2_ev[tgidx[0] % 2]]
                        last = None
                        for k in range(n_in_g):
                            m = g * 4 + k
                            last = S.emit("tensor",
                                          lambda te, bank=bank, k=k, m=m, abuf=abuf: te.transpose(
                                              bank[:, k, :],
                                              abuf[:, m * 128:(m + 1) * 128],
                                              iden_sb[:]),
                                          after=dep if k == 0 else (), inc="TC")
                        if g == ngr4 - 1:
                            ltr_ev[it] = last
                            att_last[akey] = last
                        dep2 = [last]
                        if it >= 1:
                            dep2.append(ymm_done[it - 1])
                        att_evac[g] = S.emit("vector",
                                             lambda v, bank=bank, g=g, n_in_g=n_in_g: v.tensor_copy(
                                                 attT_sb[:, g * 4:g * 4 + n_in_g, :],
                                                 bank[:, 0:n_in_g, :]),
                                             after=dep2, inc="VC")
                        tb2_ev[tgidx[0] % 2] = att_evac[g]
                        tgidx[0] += 1
                    last_y = None
                    for m in range(M):
                        jc = jlo + m
                        for h in range(2):
                            bank = p2_y0 if h == 0 else p2_y1
                            dep = [att_evac[m // 4]]
                            if m == 0:
                                dep += [y_ev[0], y_ev[1]]
                            last_y = S.emit("tensor",
                                            lambda te, bank=bank, m=m, jc=jc, h=h: te.matmul(
                                                bank[:, :], attT_sb[:, m, :],
                                                v_sb[:, jc, h * 512:(h + 1) * 512],
                                                start=(m == 0), stop=(m == M - 1)),
                                            after=dep if (m == 0 or h == 0) else (),
                                            inc="TC" if (m == M - 1 and h == 1) else None)
                    ymm_done[it] = last_y
                    yslot = it % 2
                    for h in range(2):
                        y_ev[h] = S.emit("vector",
                                         lambda v, h=h, par=par, yslot=yslot: v.tensor_scalar(
                                             y_sb[:, yslot, h * 512:(h + 1) * 512],
                                             (p2_y0 if h == 0 else p2_y1)[:, :],
                                             rinv[:, par:par + 1], None, ALU.mult),
                                         after=[last_y, rinv_ev[it], ysb_st[yslot]],
                                         inc="VC")
                    ys_spill[it] = S.dma("gpsimd", y_s[it, :, :], y_sb[:, yslot, :],
                                         after=[y_ev[0], y_ev[1]], sem="GD")
                    ysb_st[yslot] = ys_spill[it]

                for it in range(min(3, NT)):
                    qprefetch(it)
                do_s(0)
                do_exp(0)
                emitted = [0]
                exped = [0]

                def advance_s(target):
                    while emitted[0] < min(target, NT - 1):
                        emitted[0] += 1
                        if emitted[0] + 2 < NT:
                            qprefetch(emitted[0] + 2)
                        do_s(emitted[0])

                for it in range(NT):
                    advance_s(it + (1 if it < TS else 2))
                    do_ty(it)
                    while exped[0] < emitted[0]:
                        exped[0] += 1
                        do_exp(exped[0])
                    for i in range(2 * it, 2 * it + 2):
                        if i < NW2:
                            w2_mms_through(i - 4)
                            w2_chain(i)
                    w2_mms_through(2 * it - 1)
                for i in range(min(2 * NT, NW2), NW2):
                    w2_mms_through(i - 4)
                    w2_chain(i)
                w2_mms_through(NW2 - 1)
                e_alpha2 = S.emit("vector", lambda v: v.tensor_scalar(
                    alpha2_pp[:, :], p2_al[:, 0:CCH], 1.0 / H, None, ALU.mult),
                    after=[w2_mm[NW2 - 1]], inc="VC")
                ph2_snap_c = S.snap()
                ph2_snap = ph2_snap_c + S.snap_dma()
            st2.close()
        stkv.close()

        st34 = ExitStack()
        alpha2_bc = A(st34, "alpha2_bc", [128, C], BF16)
        bws1_sbA = A(st34, "bws1_sbA", [128, CCH // 2, H], BF16)
        bws1_sbB = A(st34, "bws1_sbB", [128, CCH // 2, H], BF16)

        def bws1_ap(c, lo, hi):
            return (bws1_sbA[:, c, lo:hi] if c < CCH // 2
                    else bws1_sbB[:, c - CCH // 2, lo:hi])
        bws2_sb = A(st34, "bws2_sb", [128, HCH, C], BF16)
        b2r_sb = A(st34, "b2r_sb", [1, C], BF16)
        b1p_sb = A(st34, "b1p_sb", [128, HCH], F32)
        st3 = ExitStack()
        if True:
            wo_sb = A(st3, "wo_sb", [128, CCH, C], BF16)
            bvp_sb = A(st3, "bvp_sb", [128, CCH], F32)
            bvpb_sb = A(st3, "bvpb_sb", [128, CCH], BF16)
            bo2_sb = A(st3, "bo2_sb", [1, C], BF16)
            y_in = A(st3, "y_in", [128, 3, C], BF16)
            x_in3 = A(st3, "x_in3", [128, 2, C], F32)
            yT_sb = A(st3, "yT_sb", [128, CCH, 512], BF16)
            xn_sb = A(st3, "xn_sb", [128, 2, C], F32)
            a2col = A(st3, "a2col", [128, CCH], BF16)
            a2row = A(st3, "a2row", [1, C], BF16)
            w1chunk = A(st3, "w1chunk", [128, 4, H // 4], F32)
            w1abs = A(st3, "w1abs", [128, 4, H // 4], BF16)
            p3_ta = P(st3, "p3_ta", [128, 4, 128], BF16)
            p3_tb = P(st3, "p3_tb", [128, 4, 128], BF16)
            p3_oa = P(st3, "p3_oa", [128, 512], F32)
            p3_ob = P(st3, "p3_ob", [128, 512], F32)
            p3_bcr = P(st3, "p3_bcr", [128, 1024], BF16)
            p3_bc = P(st3, "p3_bc", [128, 512], F32)
            p3_al = P(st3, "p3_al", [128, 512], F32)
            p3_al2 = P(st3, "p3_al2", [128, 512], F32)
            first3 = {"pe": ph2_snap_c, "v": ph2_snap, "s": ph2_snap,
                      "sync": ph2_snap, "gp": ph2_snap}

            def gate3(key):
                return first3[key]

            e_wo_h = [[], []]
            for cc in range(CCH):
                e_wo_h[0].append(S.dma(
                    "gpsimd", wo_sb[:, cc, 0:512],
                    woT_bf[cc * 128:(cc + 1) * 128, 0:512],
                    after=[e_wocast] + ph2_snap_c, sem="GD"))
            e_wo = None
            e_bor = S.dma("gpsimd", a2row[0:1, :], bor_h[0:1, :], sem="GD")
            e_b2r = S.dma("gpsimd", b2r_sb[0:1, :], b2r_h[0:1, :], sem="GD")
            e_bvp = S.dma("sync", bvp_sb[:, :], bvp_h[:, :], after=gate3("sync"))
            e_bvpb = S.emit("vector", lambda v: v.tensor_copy(
                bvpb_sb[:, :], bvp_sb[:, :]), after=[e_bvp] + gate3("v"), inc="VC")
            e_bo2h = [None, None]

            def emit_bo2():
                prev = None
                for h in range(2):
                    last = None
                    for cc in range(CCH):
                        last = S.emit("tensor", lambda te, cc=cc, h=h: te.matmul(
                            p3_bc[0:1, :], bvpb_sb[:, cc:cc + 1],
                            wo_sb[:, cc, h * 512:(h + 1) * 512],
                            start=(cc == 0), stop=False),
                            after=([e_bvpb, e_wo_h[h][CCH - 1], e_bor] +
                                   (gate3("pe") if h == 0 else []) +
                                   ([prev] if prev else [])) if cc == 0 else (),
                            inc=None)
                    last = S.emit("tensor", lambda te, h=h: te.matmul(
                        p3_bc[0:1, :], onesr[0:1, 0:1],
                        a2row[0:1, h * 512:(h + 1) * 512],
                        start=False, stop=True), after=[e_ones], inc="TC")
                    e_bo2h[h] = S.emit("vector", lambda v, h=h: v.tensor_copy(
                        bo2_sb[0:1, h * 512:(h + 1) * 512], p3_bc[0:1, :]),
                        after=[last], inc="VC")
                    prev = e_bo2h[h]

            e_b2w = [None] * HCH
            e_b1p = S.dma("sync", b1p_sb[:], b1p_h[:, :], after=[])

            NU = 4
            H4 = H // 4
            HH4 = HCH // 4
            NW1 = CCH * NU
            w1_ld = [None] * NW1
            w1_abs_ev = [None] * NW1
            w1_sgn_ev = [None] * NW1
            w1_spill = [None] * NW1
            w1_mm = [None] * NW1
            e_b1w = [None] * CCH

            def w1_chain(i):
                cc, u = i // NU, i % NU
                sl = i % 4
                sa = i % 4
                dep = []
                if i >= 4:
                    dep += [w1_abs_ev[i - 4], w1_sgn_ev[i - 4]]
                else:
                    dep += gate3("gp")
                w1_ld[i] = S.dma("gpsimd", w1chunk[:, sl, :],
                                 w1T_h[cc * 128:(cc + 1) * 128,
                                       u * H4:(u + 1) * H4], after=dep, sem="GD")
                d2 = [w1_ld[i]]
                if i >= 4:
                    assert w1_mm[i - 4] is not None
                    d2.append(w1_mm[i - 4])
                w1_abs_ev[i] = S.emit("vector", lambda v, sl=sl, sa=sa: v.scalar_tensor_tensor(
                    w1abs[:, sa, :], w1chunk[:, sl, :], -1.0,
                    w1chunk[:, sl, :], ALU.mult, ALU.max),
                    after=d2 + gate3("v"), inc="VC")
                d3 = [w1_ld[i]]
                w1_sgn_ev[i] = S.emit("scalar", lambda s, sl=sl, cc=cc, u=u: s.sign(
                    bws1_ap(cc, u * H4, (u + 1) * H4), w1chunk[:, sl, :]),
                    after=d3, inc="SC")
                w1_spill[i] = w1_sgn_ev[i]
                if u == NU - 1:
                    e_b1w[cc] = w1_sgn_ev[i]

            w1_aev = [None] * NW1

            def w1_mmpart(i):
                u = i % NU
                sa = i % 4
                bank = p3_al if i % 2 == 0 else p3_al2
                last = None
                dep0 = [w1_abs_ev[i], e_onesc]
                if i >= 2:
                    dep0.append(w1_aev[i - 2])
                for hl in range(HH4):
                    last = S.emit("tensor",
                                  lambda te, hl=hl, i=i, sa=sa, bank=bank: te.matmul(
                                      bank[:, hl:hl + 1],
                                      w1abs[:, sa, hl * 128:(hl + 1) * 128],
                                      onesc[:, 0:1],
                                      start=(hl == 0), stop=(hl == HH4 - 1)),
                                  after=dep0 if hl == 0 else (),
                                  inc=None if hl < HH4 - 1 else "TC")
                w1_mm[i] = last
                if i // NU == 0:
                    w1_aev[i] = S.emit("vector", lambda v, u=u, bank=bank: v.tensor_scalar(
                        alpha1_pp[:, u * HH4:(u + 1) * HH4], bank[:, 0:HH4],
                        1.0 / C, None, ALU.mult),
                        after=[last], inc="VC")
                else:
                    w1_aev[i] = S.emit("vector", lambda v, u=u, bank=bank: v.scalar_tensor_tensor(
                        alpha1_pp[:, u * HH4:(u + 1) * HH4], bank[:, 0:HH4],
                        1.0 / C, alpha1_pp[:, u * HH4:(u + 1) * HH4],
                        ALU.mult, ALU.add),
                        after=[last, w1_aev[i - NU]], inc="VC")

            e_a2c = S.emit("vector", lambda v: v.tensor_copy(
                a2col[:, :], alpha2_pp[:, :]),
                after=[e_alpha2] + gate3("v"), inc="VC")
            a2t = None
            a2bc_state = {"e_alpha2bc": None}

            def emit_a2bcast():
                a2t = None
                for cc in range(CCH):
                    a2t = S.emit("tensor", lambda te, cc=cc: te.transpose(
                        p3_bcr[0:1, cc * 128:(cc + 1) * 128],
                        a2col[:, cc:cc + 1], iden_sb[:]),
                        after=([e_a2c, e_iden] + gate3("pe")) if cc == 0 else (),
                        inc="TC" if cc == CCH - 1 else None)
                e_a2row = S.emit("vector", lambda v: v.tensor_copy(
                    a2row[0:1, :], p3_bcr[0:1, :]),
                    after=[a2t, e_bo2h[1]], inc="VC")
                e_bc = None
                for h in range(2):
                    e_bcmm = S.emit("tensor", lambda te, h=h: te.matmul(
                        p3_bc[:, :], onesr[0:1, :],
                        a2row[0:1, h * 512:(h + 1) * 512],
                        start=True, stop=True),
                        after=[e_a2row, e_ones, e_bc, e_bo2h[1]], inc="TC")
                    e_bc = S.emit("vector", lambda v, h=h: v.tensor_copy(
                        alpha2_bc[:, h * 512:(h + 1) * 512], p3_bc[:, :]),
                        after=[e_bcmm], inc="VC")
                a2bc_state["e_alpha2bc"] = e_bc


            e_b2m = [None] * HCH

            def S_emit_b2m(hh):
                e_b2w[hh] = S.dma("gpsimd", bws2_sb[:, hh, :],
                                  bws2_s[hh, :, :],
                                  after=[w2_spill[hh]], sem="GD")

            tbank3 = [p3_ta, p3_tb]
            tb3_ev = [None, None]
            ob_ev = [None, None]
            yld = [None] * NT
            xld3 = [None] * NT
            ytr_ev = [None] * NT
            xn_evac = {}
            xn_store = {}
            xn_reuse = {}

            def emit_yld(t):
                dep = [ys_spill[t]] + ph2_snap_c
                if t >= 3:
                    dep.append(ytr_ev[t - 3])
                yld[t] = S.dma("scalar", y_in[:, t % 3, :], y_s[t, :, :],
                               after=dep, sem="SD")

            for t in range(3):
                emit_yld(t)
            for cc in range(CCH):
                e_wo_h[1].append(S.dma(
                    "scalar", wo_sb[:, cc, 512:1024],
                    woT_bf[cc * 128:(cc + 1) * 128, 512:1024],
                    after=[e_wocast] + ph2_snap_c, sem="SD"))
            for t in range(2):
                xld3[t] = S.dma("sync", x_in3[:, t % 2, :],
                                x_h[t * 128:(t + 1) * 128, :], after=[])
            for sp in range(NSP):
                evac_evs = []
                for ttloc in range(4):
                    t = sp * 4 + ttloc
                    ypar = t % 3
                    lastpe, tevs = transpose_8(
                        (lambda ck, ypar=ypar: y_in[:, ypar, ck * 128:(ck + 1) * 128]),
                        yld[t], yT_sb, ttloc * 128, tbank3, tb3_ev)
                    ytr_ev[t] = lastpe
                    evac_evs += tevs
                    if t + 3 < NT:
                        emit_yld(t + 3)
                if sp == 0:
                    emit_bo2()
                for ttloc in range(4):
                    t = sp * 4 + ttloc
                    par = t % 2
                    for h in range(2):
                        bidx = (ttloc * 2 + h) % 2
                        bank = p3_oa if bidx == 0 else p3_ob
                        dep = list(evac_evs) + list(e_wo_h[h]) + \
                            [e_ones, ob_ev[bidx]]
                        for c in range(CCH):
                            S.emit("tensor",
                                   lambda te, bank=bank, c=c, ttloc=ttloc, h=h: te.matmul(
                                       bank[:, :],
                                       yT_sb[:, c, ttloc * 128:(ttloc + 1) * 128],
                                       wo_sb[:, c, h * 512:(h + 1) * 512],
                                       start=(c == 0), stop=False),
                                   after=dep if c == 0 else (), inc=None)
                        last = S.emit("tensor", lambda te, bank=bank, h=h: te.matmul(
                            bank[:, :], onesr[0:1, :],
                            bo2_sb[0:1, h * 512:(h + 1) * 512],
                            start=False, stop=True), after=[e_bo2h[h]],
                            inc="TC")
                        dep4 = [last, xld3[t], xn_reuse.get((t % 2, h))]
                        ev = S.emit("vector",
                                    lambda v, bank=bank, t=t, par=par, h=h: v.tensor_tensor(
                                        xn_sb[:, par, h * 512:(h + 1) * 512],
                                        bank[:, :],
                                        x_in3[:, t % 2, h * 512:(h + 1) * 512],
                                        ALU.add),
                                    after=dep4, inc="VC")
                        ob_ev[bidx] = ev
                        xn_evac[(t, h)] = ev
                    st = S.dma("sync", xn_s[t * 128:(t + 1) * 128, :],
                               xn_sb[:, t % 2, :],
                               after=[xn_evac[(t, 0)], xn_evac[(t, 1)]])
                    xn_reuse[(t % 2, 0)] = st
                    xn_reuse[(t % 2, 1)] = st
                    xn_store[t] = st
                    if t + 2 < NT:
                        xld3[t + 2] = S.dma(
                            "sync", x_in3[:, t % 2, :],
                            x_h[(t + 2) * 128:(t + 3) * 128, :],
                            after=[xn_evac[(t, 0)], xn_evac[(t, 1)]])
                for i in range(4 * sp, 4 * sp + 4):
                    if i < NW1:
                        if i - 4 >= 0:
                            w1_mmpart(i - 4)
                        w1_chain(i)
                for i in range(4 * (sp - 2), 4 * (sp - 2) + 4):
                    if 0 <= i < HCH:
                        S_emit_b2m(i)
                if sp == 0:
                    emit_a2bcast()
            for i in range(min(4 * NSP, NW1), NW1):
                if i - 4 >= 0:
                    w1_mmpart(i - 4)
                w1_chain(i)
            for i in range(max(NW1 - 4, 0), NW1):
                w1_mmpart(i)
            for i in range(max(4 * (NSP - 2), 0), HCH):
                S_emit_b2m(i)
            e_alpha1 = w1_aev[NW1 - 1]
            e_alpha1 = S.emit("vector", lambda v: v.engine_nop(),
                              after=w1_aev, inc="VC")
            if debug:
                S.dma("sync", dbg_a1[:, :], alpha1_pp[:, :], after=[e_alpha1])
                S.dma("gpsimd", dbg_a2[:, :], alpha2_bc[:, :],
                      after=[a2bc_state["e_alpha2bc"]], sem="GD")
                for cc in range(CCH):
                    S.dma("gpsimd", bws1_s[cc, :, :], bws1_ap(cc, 0, H),
                          after=[e_b1w[cc], e_alpha1], sem="GD")
            ph3_snap_c = S.snap()
            ph3_snap = ph3_snap_c + S.snap_dma()
        st3.close()

        st4 = ExitStack()
        if True:
            xn_in = A(st4, "xn_in", [128, 4, C], F32)
            xn_res = A(st4, "xn_res", [128, 2, C], F32)
            ln2_rg = A(st4, "ln2_rg", [128, 4, C], BF16)
            ln2T_sb = A(st4, "ln2T_sb", [128, CCH, 512], BF16)
            hT_sb = A(st4, "hT_sb", [128, HCH, 512], BF16)
            st4_sum = A(st4, "st4_sum", [128, 4], F32)
            st4_m2 = A(st4, "st4_m2", [128, 4], F32)
            st4_ssq = A(st4, "st4_ssq", [128, 4], F32)
            st4_ngm = A(st4, "st4_ngm", [128, 4], F32)
            st4_m2e = A(st4, "st4_m2e", [128, 4], F32)
            st4_rsd = A(st4, "st4_rsd", [128, 4], F32)
            p4_ha = P(st4, "p4_ha", [128, 512], F32)
            p4_hb = P(st4, "p4_hb", [128, 512], F32)
            p4_oa = P(st4, "p4_oa", [128, 512], F32)
            p4_ob = P(st4, "p4_ob", [128, 512], F32)
            p4_ta = P(st4, "p4_ta", [128, 4, 128], BF16)
            p4_tb = P(st4, "p4_tb", [128, 4, 128], BF16)
            first4 = {"pe": ph3_snap_c, "v": ph3_snap, "s": ph3_snap,
                      "sync": ph3_snap, "gp": ph3_snap}

            def gate4(key):
                return first4[key]


            tbank4 = [p4_ta, p4_tb]
            tb4_ev = [None, None]
            hbank = [p4_ha, p4_hb]
            hb_ev = [None, None]
            obank = [p4_oa, p4_ob]
            ob4_ev = [None, None]
            xnld = [None] * NT
            out_st = [None] * NT
            xrld = [None] * NT
            ln2v = [None] * NT
            sq4 = [None] * NT
            lt4 = [None] * NT
            outev = {}
            ffn2_done = {}
            stats4 = (st4_sum, st4_ssq, st4_ngm, st4_m2, st4_m2e, st4_rsd)

            span_tevs_by_blk = {}

            def emit_ln2_chains(blk):
                sums = {}
                b1 = {}
                stds = {}
                for ttloc in range(4):
                    t = blk * 4 + ttloc
                    dep = [xn_store[t]] + gate4("sync")
                    if t >= 4:
                        dep += [ln2v[t - 4], sq4[t - 4]]
                    xnld[t] = S.dma("sync", xn_in[:, t % 4, :],
                                    xn_s[t * 128:(t + 1) * 128, :], after=dep)
                    sums[t] = ln_tile_a(t, xn_in[:, t % 4, :], xnld[t], ln2_rg,
                                        stats4, lt4[t - 4] if t >= 4 else None,
                                        sq4)
                for ttloc in range(4):
                    t = blk * 4 + ttloc
                    b1[t] = ln_tile_b1(t, stats4, sums[t], sq4)
                for ttloc in range(4):
                    t = blk * 4 + ttloc
                    stds[t] = ln_tile_bstd(t, stats4, b1[t][1])
                for ttloc in range(4):
                    t = blk * 4 + ttloc
                    ln_tile_b2(t, xn_in[:, t % 4, :], xnld[t], ln2_rg, stats4,
                               b1[t][0], stds[t], ln2v, sq4)

            def emit_ln2_transposes(blk):
                span_tevs = []
                for ttloc in range(4):
                    t = blk * 4 + ttloc
                    extra = gate4("pe") if t == 0 else []
                    lastpe, tevs = transpose_8(
                        (lambda ck, t=t: ln2_rg[:, t % 4, ck * 128:(ck + 1) * 128]),
                        ln2v[t], ln2T_sb, ttloc * 128, tbank4, tb4_ev,
                        extra=extra, evac_eng="scalar")
                    lt4[t] = lastpe
                    span_tevs += tevs
                span_tevs_by_blk[blk] = span_tevs

            emit_ln2_chains(0)
            if debug:
                S.dma("gpsimd", dbg_b2sb[:, :, :], bws2_sb[:, :, :],
                      after=e_b2m, sem="GD")
                S.dma("gpsimd", dbg_a2p4[:, :], alpha2_bc[:, :],
                      after=e_b2m, sem="GD")
            emit_ln2_transposes(0)
            if NSP > 1:
                emit_ln2_chains(1)
            for hh in range(HCH):
                e_b2m[hh] = S.emit("vector", lambda v, hh=hh: v.tensor_tensor(
                    bws2_sb[:, hh, :], bws2_sb[:, hh, :], alpha2_bc[:, :],
                    ALU.mult),
                    after=[e_b2w[hh], e_b2r, a2bc_state["e_alpha2bc"]],
                    inc="VC")
            for blk in range(NSP):
                span_tevs = span_tevs_by_blk[blk]
                ffn1_evs = []
                for hh in range(HCH):
                    bank = hbank[hh % 2]
                    dep = list(span_tevs) + list(e_b1w) + [hb_ev[hh % 2]]
                    if blk == 0:
                        dep.append(e_b1w[CCH - 1])
                    last = None
                    for c in range(CCH):
                        last = S.emit("tensor",
                                      lambda te, bank=bank, c=c, hh=hh: te.matmul(
                                          bank[:, :],
                                          bws1_ap(c, hh * 128, (hh + 1) * 128),
                                          ln2T_sb[:, c, :],
                                          start=(c == 0), stop=(c == CCH - 1)),
                                      after=dep if c == 0 else (),
                                      inc=None if c < CCH - 1 else "TC")
                    ev = S.emit("scalar", lambda s, bank=bank, hh=hh: s.activation(
                        hT_sb[:, hh, :], bank[:, :], AF.Gelu,
                        bias=b1p_sb[:, hh:hh + 1], scale=alpha1_pp[:, hh:hh + 1]),
                        after=[last, e_alpha1, e_b1p] + (gate4("s") if hh == 0 else []),
                        inc="SC")
                    hb_ev[hh % 2] = ev
                    ffn1_evs.append(ev)

                if debug and blk == 0:
                    S.dma("gpsimd", dbg_hT[:, :, :], hT_sb[:, :, :],
                          after=ffn1_evs, sem="GD")
                if blk + 1 < NSP:
                    emit_ln2_transposes(blk + 1)
                if blk + 2 < NSP:
                    emit_ln2_chains(blk + 2)
                for ttloc in range(4):
                    t = blk * 4 + ttloc
                    par = t % 2
                    dep0 = [xn_store[t], out_st[t - 2] if t >= 2 else None] \
                        + gate4("gp")
                    xrld[t] = S.dma("gpsimd", xn_res[:, par, :],
                                    xn_s[t * 128:(t + 1) * 128, :], after=dep0,
                                    sem="GD")
                    for h in range(2):
                        bidx = (ttloc * 2 + h) % 2
                        bank = obank[bidx]
                        dep = list(ffn1_evs) + e_b2m + \
                            [e_b2r, e_ones, ob4_ev[bidx]]
                        for hh in range(HCH):
                            S.emit("tensor",
                                   lambda te, bank=bank, hh=hh, ttloc=ttloc, h=h: te.matmul(
                                       bank[:, :],
                                       hT_sb[:, hh, ttloc * 128:(ttloc + 1) * 128],
                                       bws2_sb[:, hh, h * 512:(h + 1) * 512],
                                       start=(hh == 0), stop=False),
                                   after=dep if hh == 0 else (), inc=None)
                        last = S.emit("tensor", lambda te, bank=bank, h=h: te.matmul(
                            bank[:, :], onesr[0:1, :],
                            b2r_sb[0:1, h * 512:(h + 1) * 512],
                            start=False, stop=True), inc="TC")
                        dep4 = [last, xrld[t], outev.get((t - 2, h))]
                        ev = S.emit("vector",
                                    lambda v, bank=bank, par=par, h=h: v.tensor_tensor(
                                        xn_res[:, par, h * 512:(h + 1) * 512],
                                        bank[:, :],
                                        xn_res[:, par, h * 512:(h + 1) * 512],
                                        ALU.add),
                                    after=dep4, inc="VC")
                        ob4_ev[bidx] = ev
                        outev[(t, h)] = ev
                    out_st[t] = S.dma("gpsimd", out_h[t * 128:(t + 1) * 128, :],
                                      xn_res[:, t % 2, :],
                                      after=[outev[(t, 0)], outev[(t, 1)]],
                                      sem="OS")
                ffn2_done[blk] = outev[(blk * 4 + 3, 1)]

            allq = []
            for i in range(S.NSEM):
                for base in ("LD", "GD", "OS", "SD"):
                    nm = f"{base}{i}"
                    if S.planned.get(nm, 0):
                        allq.append((nm, S.planned[nm]))
            S.emit("sync", lambda sy: sy.nop(), after=allq)
        st4.close()
        st34.close()

        S.check_deadlock()
        sem_stack = ExitStack()
        sems = {}
        for name in (["TC", "VC", "SC", "GC"] +
                     [f"LD{i}" for i in range(S.NSEM)] +
                     [f"GD{i}" for i in range(S.NSEM)] +
                     [f"OS{i}" for i in range(S.NSEM)] +
                     [f"SD{i}" for i in range(S.NSEM)]):
            sems[name] = sem_stack.enter_context(nc.semaphore("s" + name))
        with nc.Block() as block:
            S.replay(block, sems)
        sem_stack.close()
    stp.close()
    return nc


def make_base_inputs(Wq, bq, Wk, bk, Wv, bv, Wo, bo, w1, b1, w2, b2):
    f = lambda a: np.ascontiguousarray(np.asarray(a, dtype=np.float32))
    d = {
        "wqT": f(np.asarray(Wq).T), "wkT": f(np.asarray(Wk).T),
        "wvT": f(np.asarray(Wv).T), "woT": f(np.asarray(Wo).T),
        "w1T": f(np.asarray(w1).T), "w2T": f(np.asarray(w2).T),
        "bqp": f(np.asarray(bq).reshape(CCH, 128).T),
        "bkp": f(np.asarray(bk).reshape(CCH, 128).T),
        "b1p": f(np.asarray(b1).reshape(HCH, 128).T),
        "bvp": f(np.asarray(bv).reshape(CCH, 128).T),
        "bor": f(np.asarray(bo).reshape(1, C)),
        "b2r": f(np.asarray(b2).reshape(1, C)),
        "iden": np.eye(128, dtype=ml_dtypes.bfloat16),
    }
    a = np.arange(128)
    sub = np.where(a[None, :] >= a[:, None] + 128 - WINDOW, 0.0, NEG_INF)
    dia = np.where(a[None, :] >= a[:, None] - WINDOW, 0.0, NEG_INF)
    d["maskst"] = np.concatenate([sub, dia], axis=1).astype(np.float32)
    return d


def kernel(L0_attn_x, L0_attn_Wq, L0_attn_bq, L0_attn_Wk, L0_attn_bk,
           L0_attn_Wv, L0_attn_bv, L0_attn_Wo, L0_attn_bo,
           L0_ffn_w1, L0_ffn_b1, L0_ffn_w2, L0_ffn_b2):
    x = np.asarray(L0_attn_x)
    B, T, _ = x.shape
    nc = build_program(T)
    base = make_base_inputs(L0_attn_Wq, L0_attn_bq, L0_attn_Wk, L0_attn_bk,
                            L0_attn_Wv, L0_attn_bv, L0_attn_Wo, L0_attn_bo,
                            L0_ffn_w1, L0_ffn_b1, L0_ffn_w2, L0_ffn_b2)
    in_maps = []
    for b in range(B):
        m = dict(base)
        m["x"] = np.ascontiguousarray(x[b].astype(np.float32, copy=False))
        in_maps.append(m)
    res = run_bass_kernel_spmd(nc, in_maps, core_ids=list(range(B)))
    return np.stack([r["out"] for r in res.results]).astype(np.float32)


# version 60
# speedup vs baseline: 1.0352x; 1.0352x over previous
import numpy as np
import ml_dtypes
from contextlib import ExitStack
import concourse.bass as bass
import concourse.mybir as mybir
from concourse.bass_utils import run_bass_kernel_spmd

F32 = mybir.dt.float32
BF16 = mybir.dt.bfloat16
AF = mybir.ActivationFunctionType
ALU = mybir.AluOpType
AX = mybir.AxisListType

C = 1024
H = 4096
CCH = C // 128
HCH = H // 128
WINDOW = 16
NEG_INF = -1e30
EPS = 1e-5

ENGS = ("tensor", "vector", "scalar", "gpsimd", "sync")


class Sched:

    NSEM = 8

    def __init__(self):
        self.items = {e: [] for e in ENGS}
        self.planned = {}
        self.rr = {}

    def emit(self, eng, fn, after=(), inc=None, amt=1):
        co = {}
        for e in after:
            if e is None:
                continue
            s, v = e
            if v > 0 and v > co.get(s, 0):
                co[s] = v
        ev = None
        if inc is not None:
            self.planned[inc] = self.planned.get(inc, 0) + amt
            ev = (inc, self.planned[inc])
        self.items[eng].append((tuple(co.items()), fn, (inc, amt) if inc else None))
        return ev

    def dma(self, queue_eng, out, in_, after=(), sem="LD"):
        if sem in ("LD", "GD", "OS", "SD"):
            n = self.rr.get(sem, 0)
            self.rr[sem] = n + 1
            sem = f"{sem}{n % self.NSEM}"

        def fn(eng, out=out, in_=in_):
            return eng.dma_start(out=out, in_=in_)
        return self.emit(queue_eng, fn, after=after, inc=sem, amt=16)

    def snap(self):
        return [(s, self.planned.get(s, 0)) for s in ("TC", "VC", "SC")]

    def snap_dma(self):
        out = []
        for base in ("LD", "GD", "OS", "SD"):
            for i in range(self.NSEM):
                nm = f"{base}{i}"
                v = self.planned.get(nm, 0)
                if v:
                    out.append((nm, v))
        return out

    def check_deadlock(self):
        pos = {e: 0 for e in ENGS}
        vals = {}
        progress = True
        while progress:
            progress = False
            for e in ENGS:
                while pos[e] < len(self.items[e]):
                    after, _, inc = self.items[e][pos[e]]
                    if any(vals.get(s, 0) < v for (s, v) in after):
                        break
                    if inc:
                        s, a = inc
                        vals[s] = vals.get(s, 0) + a
                    pos[e] += 1
                    progress = True
        stuck = {e: (pos[e], len(self.items[e])) for e in ENGS
                 if pos[e] < len(self.items[e])}
        if stuck:
            det = {}
            for e in stuck:
                after, _, _ = self.items[e][pos[e]]
                det[e] = [(s, v, "planned", self.planned.get(s, 0),
                           "reached", vals.get(s, 0)) for (s, v) in after]
            raise RuntimeError(f"schedule deadlock: stuck={stuck} waits={det}")

    def replay(self, block, sems):
        waited = {e: {} for e in ENGS}

        def mk(ename):
            def body(eng):
                w = waited[ename]
                for after, fn, inc in self.items[ename]:
                    for (s, v) in after:
                        if w.get(s, 0) < v:
                            eng.wait_ge(sems[s], v)
                            w[s] = v
                    ins = fn(eng)
                    if inc is not None:
                        s, a = inc
                        ins.then_inc(sems[s], a)
            return body
        block.tensor(mk("tensor"))
        block.vector(mk("vector"))
        block.scalar(mk("scalar"))
        block.gpsimd(mk("gpsimd"))
        block.sync(mk("sync"))


def ceil_div(a, b):
    return (a + b - 1) // b


def build_program(T, debug=False):
    NT = T // 128
    NSP = T // 512
    nc = bass.Bass("TRN2", target_bir_lowering=False, debug=False)

    x_h = nc.dram_tensor("x", [T, C], F32, kind="ExternalInput").ap()
    wqT_h = nc.dram_tensor("wqT", [C, C], F32, kind="ExternalInput").ap()
    wkT_h = nc.dram_tensor("wkT", [C, C], F32, kind="ExternalInput").ap()
    wvT_h = nc.dram_tensor("wvT", [C, C], F32, kind="ExternalInput").ap()
    woT_h = nc.dram_tensor("woT", [C, C], F32, kind="ExternalInput").ap()
    w1T_h = nc.dram_tensor("w1T", [C, H], F32, kind="ExternalInput").ap()
    w2T_h = nc.dram_tensor("w2T", [H, C], F32, kind="ExternalInput").ap()
    bqp_h = nc.dram_tensor("bqp", [128, CCH], F32, kind="ExternalInput").ap()
    bkp_h = nc.dram_tensor("bkp", [128, CCH], F32, kind="ExternalInput").ap()
    b1p_h = nc.dram_tensor("b1p", [128, HCH], F32, kind="ExternalInput").ap()
    bvp_h = nc.dram_tensor("bvp", [128, CCH], F32, kind="ExternalInput").ap()
    bor_h = nc.dram_tensor("bor", [1, C], F32, kind="ExternalInput").ap()
    b2r_h = nc.dram_tensor("b2r", [1, C], F32, kind="ExternalInput").ap()
    iden_h = nc.dram_tensor("iden", [128, 128], BF16, kind="ExternalInput").ap()
    mask_h = nc.dram_tensor("maskst", [128, 256], F32, kind="ExternalInput").ap()
    out_h = nc.dram_tensor("out", [T, C], F32, kind="ExternalOutput").ap()

    KIND = "ExternalOutput" if debug else "Internal"
    qT_s = nc.dram_tensor("qT_s", [T // 128, 128, C], BF16, kind=KIND).ap()
    y_s = nc.dram_tensor("y_s", [NT, 128, C], BF16, kind=KIND).ap()
    xn_s = nc.dram_tensor("xn_s", [T, C], F32, kind=KIND).ap()
    bws1_s = nc.dram_tensor("bws1_s", [CCH, 128, H], BF16, kind=KIND).ap()
    bws2_s = nc.dram_tensor("bws2_s", [HCH, 128, C], BF16, kind=KIND).ap()
    woT_bf = nc.dram_tensor("woT_bf", [C, C], BF16, kind=KIND).ap()
    if debug:
        dbg_b2sb = nc.dram_tensor("dbg_b2sb", [128, HCH, C], F32, kind=KIND).ap()
        dbg_a2p4 = nc.dram_tensor("dbg_a2p4", [128, C], F32, kind=KIND).ap()
        dbg_hT = nc.dram_tensor("dbg_hT", [128, HCH, 512], F32, kind=KIND).ap()
        dbg_a1 = nc.dram_tensor("dbg_a1", [128, HCH], F32, kind=KIND).ap()
        dbg_a2 = nc.dram_tensor("dbg_a2", [128, C], F32, kind=KIND).ap()
        dbg_kT = nc.dram_tensor("dbg_kT", [128, CCH, T], F32, kind=KIND).ap()
        dbg_v = nc.dram_tensor("dbg_v", [NT, 128, C], F32, kind=KIND).ap()

    S = Sched()

    stp = ExitStack()
    A = lambda st, *a: st.enter_context(nc.sbuf_tensor(*a))
    P = lambda st, *a: st.enter_context(nc.psum_tensor(*a))
    if True:
        iden_sb = A(stp, "iden_sb", [128, 128], BF16)
        onesr = A(stp, "onesr", [1, 128], BF16)
        onesc = A(stp, "onesc", [128, 1], BF16)
        alpha1_pp = A(stp, "alpha1_pp", [128, HCH], F32)
        alpha2_pp = A(stp, "alpha2_pp", [128, CCH], F32)
        e_iden = S.dma("sync", iden_sb[:], iden_h[:, :])
        e_ones = S.emit("gpsimd", lambda g: g.memset(onesr[:, :], 1.0), inc="GC")
        e_onesc = S.emit("gpsimd", lambda g: g.memset(onesc[:, :], 1.0), inc="GC")

        stkv = ExitStack()
        if True:
            kT_sb = A(stkv, "kT_sb", [128, CCH, T], BF16)
            v_sb = A(stkv, "v_sb", [128, NT, C], BF16)
            st1 = ExitStack()
            if True:
                wq_sb = A(st1, "wq_sb", [128, CCH, C], BF16)
                wk_sb = A(st1, "wk_sb", [128, CCH, C], BF16)
                wv_sb = A(st1, "wv_sb", [128, CCH, C], BF16)
                x_in = A(st1, "x_in", [128, 3, C], F32)
                ln_rg = A(st1, "ln_rg", [128, 4, C], BF16)
                lnT_sb = A(st1, "lnT_sb", [128, CCH, 512], BF16)
                qsp = A(st1, "qsp", [128, 1, 512], BF16)
                bqp_sb = A(st1, "bqp_sb", [128, CCH], F32)
                bkp_sb = A(st1, "bkp_sb", [128, CCH], F32)
                st_sum = A(st1, "st_sum", [128, 4], F32)
                st_m2 = A(st1, "st_m2", [128, 4], F32)
                st_ssq = A(st1, "st_ssq", [128, 4], F32)
                st_ngm = A(st1, "st_ngm", [128, 4], F32)
                st_m2e = A(st1, "st_m2e", [128, 4], F32)
                st_rsd = A(st1, "st_rsd", [128, 4], F32)
                p1_ka = P(st1, "p1_ka", [128, 512], F32)
                p1_kb = P(st1, "p1_kb", [128, 512], F32)
                p1_qa = P(st1, "p1_qa", [128, 512], F32)
                p1_qb = P(st1, "p1_qb", [128, 512], F32)
                p1_va = P(st1, "p1_va", [128, 512], F32)
                p1_vb = P(st1, "p1_vb", [128, 512], F32)
                p1_ta = P(st1, "p1_ta", [128, 4, 128], BF16)
                p1_tb = P(st1, "p1_tb", [128, 4, 128], BF16)
                e_bqp = S.dma("sync", bqp_sb[:], bqp_h[:, :])
                e_bkp = S.dma("sync", bkp_sb[:], bkp_h[:, :])
                e_wq, e_wk, e_wv = [], [], []
                for (wsb, wh, evl) in ((wk_sb, wkT_h, e_wk), (wq_sb, wqT_h, e_wq),
                                       (wv_sb, wvT_h, e_wv)):
                    for cc in range(CCH):
                        evl.append(S.dma("gpsimd", wsb[:, cc, :],
                                         wh[cc * 128:(cc + 1) * 128, :], sem="GD"))

                xld = [None] * NT
                lnv = [None] * NT
                sqe = [None] * NT
                ltev = [None] * NT
                kev_all = []
                qspill = {}

                kbank = [p1_ka, p1_kb]
                qbank = [p1_qa, p1_qb]
                vbank = [p1_va, p1_vb]
                tbank = [p1_ta, p1_tb]
                kb_ev = [None, None]
                qb_ev = [None, None]
                vb_ev = [None, None]
                tb_ev = [None, None]
                qsp_ev = [None, None]

                def ln_tile_sum(t, x_ap, ld_ev, stats):
                    (ssum, sssq, sngm, sm2, sm2e, srsd) = stats
                    par = t % 4
                    return S.emit("vector", lambda v, par=par: v.tensor_reduce(
                        ssum[:, par:par + 1], x_ap, AX.X, ALU.add),
                        after=[ld_ev], inc="VC")

                def ln_tile_sq(t, x_ap, ld_ev, ring, stats, lnconsumed,
                               sqe_arr):
                    (ssum, sssq, sngm, sm2, sm2e, srsd) = stats
                    par = t % 4
                    dep_sq = [ld_ev]
                    if lnconsumed is not None:
                        dep_sq.append(lnconsumed)
                    sqe_arr[t] = S.emit("scalar", lambda s, par=par: s.activation(
                        ring[:, par, :], x_ap, AF.Square, bias=0.0, scale=1.0,
                        accum_out=sssq[:, par:par + 1]),
                        after=dep_sq, inc="SC")
                    return sqe_arr[t]

                def ln_tile_a(t, x_ap, ld_ev, ring, stats, lnconsumed,
                              sqe_arr):
                    e_sum = ln_tile_sum(t, x_ap, ld_ev, stats)
                    ln_tile_sq(t, x_ap, ld_ev, ring, stats, lnconsumed, sqe_arr)
                    return e_sum

                def ln_tile_b1(t, stats, e_sum, sqe_arr):
                    (ssum, sssq, sngm, sm2, sm2e, srsd) = stats
                    par = t % 4
                    e_ngm = S.emit("vector", lambda v, par=par: v.tensor_scalar(
                        sngm[:, par:par + 1], ssum[:, par:par + 1],
                        -1.0 / C, None, ALU.mult),
                        after=[e_sum], inc="VC")
                    e_m2 = S.emit("vector", lambda v, par=par: v.scalar_tensor_tensor(
                        sm2[:, par:par + 1], sngm[:, par:par + 1], 1.0,
                        sngm[:, par:par + 1], ALU.mult, ALU.mult),
                        after=[e_ngm], inc="VC")
                    e_m2e = S.emit("vector", lambda v, par=par: v.tensor_scalar(
                        sm2e[:, par:par + 1], sm2[:, par:par + 1], -1.0, EPS,
                        ALU.mult, ALU.add),
                        after=[e_m2], inc="VC")
                    e_vep = S.emit("vector", lambda v, par=par: v.tensor_scalar(
                        sm2[:, par:par + 1], sssq[:, par:par + 1], 1.0 / C,
                        sm2e[:, par:par + 1], ALU.mult, ALU.add),
                        after=[sqe_arr[t], e_m2e], inc="VC")
                    return e_ngm, e_vep

                def ln_tile_bstd(t, stats, e_vep):
                    (ssum, sssq, sngm, sm2, sm2e, srsd) = stats
                    par = t % 4
                    return S.emit("scalar", lambda s, par=par: s.activation(
                        sm2e[:, par:par + 1], sm2[:, par:par + 1], AF.Sqrt,
                        bias=0.0, scale=1.0),
                        after=[e_vep], inc="SC")

                def ln_tile_b2(t, x_ap, ld_ev, ring, stats, e_ngm, e_std,
                               lnv_arr, sqe_arr, ring_guard=None):
                    (ssum, sssq, sngm, sm2, sm2e, srsd) = stats
                    par = t % 4
                    e_rsd = S.emit("vector", lambda v, par=par: v.reciprocal(
                        srsd[:, par:par + 1], sm2e[:, par:par + 1]),
                        after=[e_std], inc="VC")
                    dep = [e_ngm, e_rsd, ld_ev, sqe_arr[t], ring_guard]
                    lnv_arr[t] = S.emit("vector", lambda v, par=par: v.tensor_scalar(
                        ring[:, par, :], x_ap,
                        sngm[:, par:par + 1], srsd[:, par:par + 1],
                        ALU.add, ALU.mult),
                        after=dep, inc="VC")

                def transpose_8(src_fn, src_ev, dest_sb, dest_off, banks, b_ev,
                                extra=(), evac_eng="vector"):
                    evacs = []
                    lastpe = None
                    inc_sem = "VC" if evac_eng == "vector" else "SC"
                    for g in range(2):
                        bank = banks[g]
                        dep = [src_ev, e_iden, b_ev[g]] + list(extra)
                        last = None
                        for k in range(4):
                            ck = g * 4 + k
                            last = S.emit("tensor",
                                          lambda te, bank=bank, k=k, ck=ck: te.transpose(
                                              bank[:, k, :], src_fn(ck), iden_sb[:]),
                                          after=dep if k == 0 else (), inc="TC")
                        if evac_eng == "vector":
                            ev = S.emit("vector",
                                        lambda v, bank=bank, g=g, dest_off=dest_off,
                                        dest_sb=dest_sb: v.tensor_copy(
                                            dest_sb[:, g * 4:(g + 1) * 4,
                                                    dest_off:dest_off + 128],
                                            bank[:, 0:4, :]),
                                        after=[last], inc=inc_sem)
                        else:
                            ev = S.emit("scalar",
                                        lambda s, bank=bank, g=g, dest_off=dest_off,
                                        dest_sb=dest_sb: s.activation(
                                            dest_sb[:, g * 4:(g + 1) * 4,
                                                    dest_off:dest_off + 128],
                                            bank[:, 0:4, :], AF.Copy,
                                            bias=0.0, scale=1.0),
                                        after=[last], inc=inc_sem)
                        b_ev[g] = ev
                        evacs.append(ev)
                        lastpe = last
                    return lastpe, evacs

                stats1 = (st_sum, st_ssq, st_ngm, st_m2, st_m2e, st_rsd)

                sums = {}
                b1ev = {}

                def _xld_sum(t, dep):
                    xld[t] = S.dma("sync", x_in[:, t % 3, :],
                                   x_h[t * 128:(t + 1) * 128, :], after=dep)
                    sums[t] = ln_tile_sum(t, x_in[:, t % 3, :], xld[t], stats1)

                def _sq(t):
                    g1 = t // 4
                    sqe[t] = S.emit("scalar", lambda s, t=t, g1=g1: s.activation(
                        kT_sb[:, 0:2, g1 * 512:g1 * 512 + 512],
                        x_in[:, t % 3, :], AF.Square, bias=0.0, scale=1.0,
                        accum_out=st_ssq[:, (t % 4):(t % 4) + 1]),
                        after=[xld[t]], inc="SC")

                def _b2(t):
                    ln_tile_b2(t, x_in[:, t % 3, :], xld[t], ln_rg, stats1,
                               b1ev[t][0], stds1[t], lnv, sqe,
                               ltev[t - 4] if t >= 4 else None)

                stds1 = {}

                def chains_A(sp):
                    for t in range(sp * 4, sp * 4 + 3):
                        dep = []
                        if t >= 3:
                            dep = [lnv[t - 3], sqe[t - 3]]
                        _xld_sum(t, dep)

                def chain_fin(sp, j):
                    t0 = sp * 4
                    t3 = t0 + 3
                    if j == 0:
                        for t in range(t0, t0 + 3):
                            _sq(t)
                    elif j == 1:
                        for t in range(t0, t0 + 3):
                            b1ev[t] = ln_tile_b1(t, stats1, sums[t], sqe)
                        stds1[t0] = ln_tile_bstd(t0, stats1, b1ev[t0][1])
                        _b2(t0)
                    elif j == 2:
                        stds1[t0 + 1] = ln_tile_bstd(t0 + 1, stats1,
                                                     b1ev[t0 + 1][1])
                        _b2(t0 + 1)
                        stds1[t0 + 2] = ln_tile_bstd(t0 + 2, stats1,
                                                     b1ev[t0 + 2][1])
                        _b2(t0 + 2)
                        _xld_sum(t3, [lnv[t3 - 3], sqe[t3 - 3]])
                        _sq(t3)
                    else:
                        b1ev[t3] = ln_tile_b1(t3, stats1, sums[t3], sqe)
                        stds1[t3] = ln_tile_bstd(t3, stats1, b1ev[t3][1])
                        _b2(t3)

                chains_A(0)
                for j in range(4):
                    chain_fin(0, j)
                for sp in range(NSP):
                    evac_evs = []
                    for ttloc in range(4):
                        t = sp * 4 + ttloc
                        lt, tevs = transpose_8(
                            (lambda ck, t=t: ln_rg[:, t % 4, ck * 128:(ck + 1) * 128]),
                            lnv[t], lnT_sb, ttloc * 128, tbank, tb_ev)
                        ltev[t] = lt
                        evac_evs += tevs
                    if sp + 1 < NSP:
                        chains_A(sp + 1)
                    for cc in range(CCH):
                        for which in ("k", "q"):
                            bank = (kbank if which == "k" else qbank)[cc % 2]
                            bev = kb_ev if which == "k" else qb_ev
                            wsb = wk_sb if which == "k" else wq_sb
                            wld = e_wk if which == "k" else e_wq
                            dep = list(evac_evs) + list(wld) + [bev[cc % 2]]
                            last = None
                            for c in range(CCH):
                                last = S.emit("tensor",
                                              lambda te, bank=bank, c=c, cc=cc, wsb=wsb: te.matmul(
                                                  bank[:, :],
                                                  wsb[:, c, cc * 128:(cc + 1) * 128],
                                                  lnT_sb[:, c, :],
                                                  start=(c == 0), stop=(c == CCH - 1)),
                                              after=dep if c == 0 else (),
                                              inc=None if c < CCH - 1 else "TC")
                            if which == "k":
                                ev = S.emit("scalar",
                                            lambda s, bank=bank, cc=cc, sp=sp: s.activation(
                                                kT_sb[:, cc, sp * 512:(sp + 1) * 512],
                                                bank[:, :], AF.Identity,
                                                bias=bkp_sb[:, cc:cc + 1], scale=1.0),
                                            after=[last, e_bkp], inc="SC")
                                kb_ev[cc % 2] = ev
                                kev_all.append(ev)
                            else:
                                qslot = 0
                                ev = S.emit("scalar",
                                            lambda s, bank=bank, cc=cc, qslot=qslot: s.activation(
                                                qsp[:, qslot, :], bank[:, :], AF.Identity,
                                                bias=bqp_sb[:, cc:cc + 1], scale=1.0),
                                            after=[last, e_bqp, qsp_ev[qslot]],
                                            inc="SC")
                                qb_ev[cc % 2] = ev
                                qspill[(cc, sp)] = S.dma(
                                    "gpsimd",
                                    qT_s[sp * 4:(sp + 1) * 4, :,
                                         cc * 128:(cc + 1) * 128]
                                    .rearrange("a p b -> p a b"),
                                    qsp[:, qslot, :].rearrange(
                                        "p (a b) -> p a b", a=4),
                                    after=[ev], sem="GD")
                                qsp_ev[qslot] = qspill[(cc, sp)]
                    vring = [vbank[0], vbank[1], kbank[0], kbank[1]]
                    vre = [vb_ev[0], vb_ev[1], kb_ev[0], kb_ev[1]]
                    for ttloc in range(4):
                        t = sp * 4 + ttloc
                        for h in range(2):
                            bidx = (ttloc * 2 + h) % 4
                            bank = vring[bidx]
                            dep = list(evac_evs) + list(e_wv) + [vre[bidx]]
                            last = None
                            for c in range(CCH):
                                last = S.emit("tensor",
                                              lambda te, bank=bank, c=c, ttloc=ttloc, h=h: te.matmul(
                                                  bank[:, :],
                                                  lnT_sb[:, c, ttloc * 128:(ttloc + 1) * 128],
                                                  wv_sb[:, c, h * 512:(h + 1) * 512],
                                                  start=(c == 0), stop=(c == CCH - 1)),
                                              after=dep if c == 0 else (),
                                              inc=None if c < CCH - 1 else "TC")
                            ev = S.emit("scalar",
                                        lambda s, bank=bank, t=t, h=h: s.activation(
                                            v_sb[:, t, h * 512:(h + 1) * 512],
                                            bank[:, :], AF.Copy, bias=0.0,
                                            scale=1.0),
                                        after=[last], inc="SC")
                            vre[bidx] = ev
                        if sp + 1 < NSP:
                            chain_fin(sp + 1, ttloc)
                    vb_ev[0], vb_ev[1] = vre[0], vre[1]
                    kb_ev[0], kb_ev[1] = vre[2], vre[3]

                ph1_snap_c = S.snap()
                ph1_snap = ph1_snap_c + S.snap_dma()
                if debug:
                    S.dma("gpsimd", dbg_kT[:, :, :], kT_sb[:, :, :],
                          after=ph1_snap, sem="GD")
                    for t in range(NT):
                        S.dma("gpsimd", dbg_v[t, :, :], v_sb[:, t, :],
                              after=ph1_snap, sem="GD")
            st1.close()

            SW = NT * 128
            st2 = ExitStack()
            if True:
                mask_sb = A(st2, "mask_sb", [128, 256], BF16)
                s_sb = A(st2, "s_sb", [128, 2, SW], F32)
                att_sb = A(st2, "att_sb", [128, SW], BF16)
                TS = NT - 8 if NT >= 24 else NT
                att2_sb = A(st2, "att2_sb", [128, (NT - TS) * 128], BF16) \
                    if TS < NT else None
                attT_sb = A(st2, "attT_sb", [128, NT, 128], BF16)
                q_in = A(st2, "q_in", [128, 3, C], BF16)
                y_sb = A(st2, "y_sb", [128, 2, C], BF16)
                gmax = A(st2, "gmax", [128, 2, 8], F32)
                ngmx = A(st2, "ngmx", [128, 2], F32)
                lsum = A(st2, "lsum", [128, 2], F32)
                rinv = A(st2, "rinv", [128, 2], F32)
                w2chunk = A(st2, "w2chunk", [128, 2, C], F32)
                w2abs = A(st2, "w2abs", [128, 4, C], BF16)
                w2sgn = A(st2, "w2sgn", [128, 1, C], BF16)
                p2_sa = P(st2, "p2_sa", [128, 512], F32)
                p2_sb = P(st2, "p2_sb", [128, 512], F32)
                p2_ta = P(st2, "p2_ta", [128, 4, 128], BF16)
                p2_tb = P(st2, "p2_tb", [128, 4, 128], BF16)
                p2_y0 = P(st2, "p2_y0", [128, 512], F32)
                p2_y1 = P(st2, "p2_y1", [128, 512], F32)
                p2_al = P(st2, "p2_al", [128, 512], F32)
                p2_sc = P(st2, "p2_sc", [128, 512], F32)
                sbank = [p2_sa, p2_sb, p2_sc]
                tbank2 = [p2_ta, p2_tb]
                sb_ev = [None, None, None]
                tb2_ev = [None, None]
                y_ev = [None, None]
                ysb_st = [None, None]
                qld = [None] * NT
                exp_ev = [None] * NT
                ltr_ev = [None] * NT
                att_last = {0: None, 1: None}
                ymm_done = [None] * NT
                smm_last = [None] * NT
                sgrp_evac = [None] * NT
                maskadd = [None] * NT
                ngx_ev = [None] * NT
                rinv_ev = [None] * NT
                ys_spill = [None] * NT
                e_wocast = S.dma("gpsimd", woT_bf[:, :], woT_h[:, :],
                                 sem="GD")
                e_mask = S.dma("gpsimd", mask_sb[:], mask_h[:, :],
                               after=ph1_snap, sem="GD")
                gidx = [0]
                tgidx = [0]
                first2 = {"pe": ph1_snap_c, "v": ph1_snap, "s": ph1_snap,
                          "sync": ph1_snap, "gp": ph1_snap}

                def gate(key):
                    return first2[key]

                NW2 = HCH
                w2_ld = [None] * NW2
                w2_abs_ev = [None] * NW2
                w2_sgn_ev = [None] * NW2
                w2_spill = [None] * NW2
                w2_mm = [None] * NW2

                def w2_chain(i):
                    sl = i % 2
                    sa = i % 4
                    dep = gate("sync")
                    if i >= 2:
                        dep += [w2_abs_ev[i - 2], w2_sgn_ev[i - 2]]
                    w2_ld[i] = S.dma("sync", w2chunk[:, sl, :],
                                     w2T_h[i * 128:(i + 1) * 128, :], after=dep)
                    d2 = [w2_ld[i]]
                    if i >= 4:
                        assert w2_mm[i - 4] is not None
                        d2.append(w2_mm[i - 4])
                    w2_abs_ev[i] = S.emit("scalar", lambda s, sl=sl, sa=sa: s.activation(
                        w2abs[:, sa, :], w2chunk[:, sl, :], AF.Abs,
                        bias=0.0, scale=1.0), after=d2, inc="SC")
                    d3 = [w2_ld[i]]
                    if i >= 1:
                        d3.append(w2_spill[i - 1])
                    w2_sgn_ev[i] = S.emit("scalar", lambda s, sl=sl: s.sign(
                        w2sgn[:, 0, :], w2chunk[:, sl, :]), after=d3, inc="SC")
                    w2_spill[i] = S.dma("sync", bws2_s[i, :, :],
                                        w2sgn[:, 0, :], after=[w2_sgn_ev[i]])

                w2_next_mm = [0]

                def w2_mms_through(k):
                    while w2_next_mm[0] <= min(k, NW2 - 1):
                        w2_mmpart(w2_next_mm[0])
                        w2_next_mm[0] += 1

                def w2_mmpart(i):
                    sa = i % 4
                    last = None
                    dep4 = [w2_abs_ev[i], e_onesc] + (gate("pe") if i == 0 else [])
                    for cp in range(CCH):
                        last = S.emit("tensor",
                                      lambda te, cp=cp, i=i, sa=sa: te.matmul(
                                          p2_al[:, cp:cp + 1],
                                          w2abs[:, sa, cp * 128:(cp + 1) * 128],
                                          onesc[:, 0:1],
                                          start=(i == 0 and cp == 0),
                                          stop=(i == NW2 - 1 and cp == CCH - 1)),
                                      after=dep4 if cp == 0 else (),
                                      inc=None if cp < CCH - 1 else "TC")
                    w2_mm[i] = last

                def qprefetch(it):
                    dep = [qspill[(cc, it // 4)] for cc in range(CCH)]
                    dep += gate("gp")
                    if it >= 3:
                        dep.append(smm_last[it - 3])
                    qld[it] = S.dma("gpsimd", q_in[:, it % 3, :],
                                    qT_s[it, :, :], after=dep, sem="GD")

                def do_s(it):
                    par = it % 2
                    jlo = max(it - 1, 0)
                    M = NT - jlo
                    ngr = ceil_div(M, 4)
                    grp = []
                    for g in range(ngr):
                        c0 = g * 4
                        w = min(4, M - c0) * 128
                        bank = sbank[gidx[0] % 3]
                        dep = [qld[it], sb_ev[gidx[0] % 3]] + kev_all
                        if it == 0 and g == 0:
                            dep += gate("pe")
                        last = None
                        for c in range(CCH):
                            last = S.emit("tensor",
                                          lambda te, bank=bank, c=c, it=it, jlo=jlo, c0=c0, w=w: te.matmul(
                                              bank[:, 0:w],
                                              q_in[:, it % 3, c * 128:(c + 1) * 128],
                                              kT_sb[:, c, (jlo + c0) * 128:(jlo + c0) * 128 + w],
                                              start=(c == 0), stop=(c == CCH - 1)),
                                          after=dep if c == 0 else (),
                                          inc=None if c < CCH - 1 else "TC")
                        smm_last[it] = last
                        dep2 = [last] + gate("v")
                        if it >= 2:
                            dep2.append(exp_ev[it - 2])
                        ev = S.emit("vector",
                                    lambda v, bank=bank, par=par, c0=c0, w=w, g=g: v.tensor_scalar(
                                        s_sb[:, par, c0 * 128:c0 * 128 + w],
                                        bank[:, 0:w], 0.03125, None,
                                        ALU.mult, ALU.max,
                                        accum_out=gmax[:, par, g:g + 1]),
                                    after=dep2, inc="VC")
                        sb_ev[gidx[0] % 3] = ev
                        grp.append(ev)
                        gidx[0] += 1
                    sgrp_evac[it] = grp
                    if it == 0:
                        maskadd[it] = S.emit("vector", lambda v: v.scalar_tensor_tensor(
                            s_sb[:, 0, 0:128], s_sb[:, 0, 0:128], 1.0,
                            mask_sb[:, 128:256], ALU.mult, ALU.add),
                            after=[grp[0], e_mask], inc="VC")
                    else:
                        maskadd[it] = S.emit("vector",
                                             lambda v, par=par: v.scalar_tensor_tensor(
                                                 s_sb[:, par, 0:256],
                                                 s_sb[:, par, 0:256], 1.0,
                                                 mask_sb[:, 0:256], ALU.mult, ALU.add),
                                             after=[grp[0], e_mask], inc="VC")
                    ngx_ev[it] = S.emit("vector",
                                        lambda v, par=par, ngr=ngr: v.tensor_reduce(
                                            ngmx[:, par:par + 1],
                                            gmax[:, par, 0:ngr], AX.X, ALU.max,
                                            negate=True),
                                        after=grp, inc="VC")

                def att_of(it):
                    if it >= TS and it % 2 == 1:
                        return att2_sb, 1
                    return att_sb, 0

                def do_exp(it):
                    par = it % 2
                    jlo = max(it - 1, 0)
                    L = (NT - jlo) * 128
                    abuf, akey = att_of(it)
                    dep = list(sgrp_evac[it]) + [maskadd[it], ngx_ev[it]] + gate("s")
                    dep.append(att_last[akey])
                    exp_ev[it] = S.emit("scalar",
                                        lambda s, par=par, L=L, abuf=abuf: s.activation(
                                            abuf[:, 0:L], s_sb[:, par, 0:L],
                                            AF.Exp, bias=ngmx[:, par:par + 1],
                                            scale=1.0,
                                            accum_out=lsum[:, par:par + 1]),
                                        after=dep, inc="SC")
                    rinv_ev[it] = S.emit("vector", lambda v, par=par: v.reciprocal(
                        rinv[:, par:par + 1], lsum[:, par:par + 1]),
                        after=[exp_ev[it]], inc="VC")

                def do_ty(it):
                    par = it % 2
                    jlo = max(it - 1, 0)
                    M = NT - jlo
                    ngr4 = ceil_div(M, 4)
                    abuf, akey = att_of(it)
                    att_evac = [None] * ngr4
                    for g in range(ngr4):
                        n_in_g = min(4, M - g * 4)
                        bank = tbank2[tgidx[0] % 2]
                        dep = [exp_ev[it], e_iden, tb2_ev[tgidx[0] % 2]]
                        last = None
                        for k in range(n_in_g):
                            m = g * 4 + k
                            last = S.emit("tensor",
                                          lambda te, bank=bank, k=k, m=m, abuf=abuf: te.transpose(
                                              bank[:, k, :],
                                              abuf[:, m * 128:(m + 1) * 128],
                                              iden_sb[:]),
                                          after=dep if k == 0 else (), inc="TC")
                        if g == ngr4 - 1:
                            ltr_ev[it] = last
                            att_last[akey] = last
                        dep2 = [last]
                        if it >= 1:
                            dep2.append(ymm_done[it - 1])
                        att_evac[g] = S.emit("vector",
                                             lambda v, bank=bank, g=g, n_in_g=n_in_g: v.tensor_copy(
                                                 attT_sb[:, g * 4:g * 4 + n_in_g, :],
                                                 bank[:, 0:n_in_g, :]),
                                             after=dep2, inc="VC")
                        tb2_ev[tgidx[0] % 2] = att_evac[g]
                        tgidx[0] += 1
                    last_y = None
                    for m in range(M):
                        jc = jlo + m
                        for h in range(2):
                            bank = p2_y0 if h == 0 else p2_y1
                            dep = [att_evac[m // 4]]
                            if m == 0:
                                dep += [y_ev[0], y_ev[1]]
                            last_y = S.emit("tensor",
                                            lambda te, bank=bank, m=m, jc=jc, h=h: te.matmul(
                                                bank[:, :], attT_sb[:, m, :],
                                                v_sb[:, jc, h * 512:(h + 1) * 512],
                                                start=(m == 0), stop=(m == M - 1)),
                                            after=dep if (m == 0 or h == 0) else (),
                                            inc="TC" if (m == M - 1 and h == 1) else None)
                    ymm_done[it] = last_y
                    yslot = it % 2
                    for h in range(2):
                        y_ev[h] = S.emit("vector",
                                         lambda v, h=h, par=par, yslot=yslot: v.tensor_scalar(
                                             y_sb[:, yslot, h * 512:(h + 1) * 512],
                                             (p2_y0 if h == 0 else p2_y1)[:, :],
                                             rinv[:, par:par + 1], None, ALU.mult),
                                         after=[last_y, rinv_ev[it], ysb_st[yslot]],
                                         inc="VC")
                    ys_spill[it] = S.dma("gpsimd", y_s[it, :, :], y_sb[:, yslot, :],
                                         after=[y_ev[0], y_ev[1]], sem="GD")
                    ysb_st[yslot] = ys_spill[it]

                for it in range(min(3, NT)):
                    qprefetch(it)
                do_s(0)
                do_exp(0)
                emitted = [0]
                exped = [0]

                def advance_s(target):
                    while emitted[0] < min(target, NT - 1):
                        emitted[0] += 1
                        if emitted[0] + 2 < NT:
                            qprefetch(emitted[0] + 2)
                        do_s(emitted[0])

                for it in range(NT):
                    advance_s(it + (1 if it < TS else 2))
                    do_ty(it)
                    while exped[0] < emitted[0]:
                        exped[0] += 1
                        do_exp(exped[0])
                    for i in range(2 * it, 2 * it + 2):
                        if i < NW2:
                            w2_mms_through(i - 4)
                            w2_chain(i)
                    w2_mms_through(2 * it - 1)
                for i in range(min(2 * NT, NW2), NW2):
                    w2_mms_through(i - 4)
                    w2_chain(i)
                w2_mms_through(NW2 - 1)
                e_alpha2 = S.emit("vector", lambda v: v.tensor_scalar(
                    alpha2_pp[:, :], p2_al[:, 0:CCH], 1.0 / H, None, ALU.mult),
                    after=[w2_mm[NW2 - 1]], inc="VC")
                ph2_snap_c = S.snap()
                ph2_snap = ph2_snap_c + S.snap_dma()
            st2.close()
        stkv.close()

        st34 = ExitStack()
        alpha2_bc = A(st34, "alpha2_bc", [128, C], BF16)
        bws1_sbA = A(st34, "bws1_sbA", [128, CCH // 2, H], BF16)
        bws1_sbB = A(st34, "bws1_sbB", [128, CCH // 2, H], BF16)

        def bws1_ap(c, lo, hi):
            return (bws1_sbA[:, c, lo:hi] if c < CCH // 2
                    else bws1_sbB[:, c - CCH // 2, lo:hi])
        bws2_sb = A(st34, "bws2_sb", [128, HCH, C], BF16)
        b2r_sb = A(st34, "b2r_sb", [1, C], BF16)
        b1p_sb = A(st34, "b1p_sb", [128, HCH], F32)
        st3 = ExitStack()
        if True:
            wo_sb = A(st3, "wo_sb", [128, CCH, C], BF16)
            bvp_sb = A(st3, "bvp_sb", [128, CCH], F32)
            bvpb_sb = A(st3, "bvpb_sb", [128, CCH], BF16)
            bo2_sb = A(st3, "bo2_sb", [1, C], BF16)
            y_in = A(st3, "y_in", [128, 3, C], BF16)
            x_in3 = A(st3, "x_in3", [128, 2, C], F32)
            yT_sb = A(st3, "yT_sb", [128, CCH, 512], BF16)
            xn_sb = A(st3, "xn_sb", [128, 2, C], F32)
            a2col = A(st3, "a2col", [128, CCH], BF16)
            a2row = A(st3, "a2row", [1, C], BF16)
            w1chunk = A(st3, "w1chunk", [128, 4, H // 4], F32)
            w1abs = A(st3, "w1abs", [128, 4, H // 4], BF16)
            p3_ta = P(st3, "p3_ta", [128, 4, 128], BF16)
            p3_tb = P(st3, "p3_tb", [128, 4, 128], BF16)
            p3_oa = P(st3, "p3_oa", [128, 512], F32)
            p3_ob = P(st3, "p3_ob", [128, 512], F32)
            p3_bcr = P(st3, "p3_bcr", [128, 1024], BF16)
            p3_bc = P(st3, "p3_bc", [128, 512], F32)
            p3_al = P(st3, "p3_al", [128, 512], F32)
            p3_al2 = P(st3, "p3_al2", [128, 512], F32)
            first3 = {"pe": ph2_snap_c, "v": ph2_snap, "s": ph2_snap,
                      "sync": ph2_snap, "gp": ph2_snap}

            def gate3(key):
                return first3[key]

            e_wo_h = [[], []]
            for cc in range(CCH):
                e_wo_h[0].append(S.dma(
                    "gpsimd", wo_sb[:, cc, 0:512],
                    woT_bf[cc * 128:(cc + 1) * 128, 0:512],
                    after=[e_wocast] + ph2_snap_c, sem="GD"))
            e_wo = None
            e_bor = S.dma("gpsimd", a2row[0:1, :], bor_h[0:1, :], sem="GD")
            e_b2r = S.dma("gpsimd", b2r_sb[0:1, :], b2r_h[0:1, :], sem="GD")
            e_bvp = S.dma("sync", bvp_sb[:, :], bvp_h[:, :], after=gate3("sync"))
            e_bvpb = S.emit("vector", lambda v: v.tensor_copy(
                bvpb_sb[:, :], bvp_sb[:, :]), after=[e_bvp] + gate3("v"), inc="VC")
            e_bo2h = [None, None]

            def emit_bo2():
                prev = None
                for h in range(2):
                    last = None
                    for cc in range(CCH):
                        last = S.emit("tensor", lambda te, cc=cc, h=h: te.matmul(
                            p3_bc[0:1, :], bvpb_sb[:, cc:cc + 1],
                            wo_sb[:, cc, h * 512:(h + 1) * 512],
                            start=(cc == 0), stop=False),
                            after=([e_bvpb, e_wo_h[h][CCH - 1], e_bor] +
                                   (gate3("pe") if h == 0 else []) +
                                   ([prev] if prev else [])) if cc == 0 else (),
                            inc=None)
                    last = S.emit("tensor", lambda te, h=h: te.matmul(
                        p3_bc[0:1, :], onesr[0:1, 0:1],
                        a2row[0:1, h * 512:(h + 1) * 512],
                        start=False, stop=True), after=[e_ones], inc="TC")
                    e_bo2h[h] = S.emit("vector", lambda v, h=h: v.tensor_copy(
                        bo2_sb[0:1, h * 512:(h + 1) * 512], p3_bc[0:1, :]),
                        after=[last], inc="VC")
                    prev = e_bo2h[h]

            e_b2w = [None] * HCH
            e_b1p = S.dma("sync", b1p_sb[:], b1p_h[:, :], after=[])

            NU = 4
            H4 = H // 4
            HH4 = HCH // 4
            NW1 = CCH * NU
            w1_ld = [None] * NW1
            w1_abs_ev = [None] * NW1
            w1_sgn_ev = [None] * NW1
            w1_spill = [None] * NW1
            w1_mm = [None] * NW1
            e_b1w = [None] * CCH

            def w1_chain(i):
                cc, u = i // NU, i % NU
                sl = i % 4
                sa = i % 4
                dep = []
                if i >= 4:
                    dep += [w1_abs_ev[i - 4], w1_sgn_ev[i - 4]]
                else:
                    dep += gate3("gp")
                w1_ld[i] = S.dma("gpsimd", w1chunk[:, sl, :],
                                 w1T_h[cc * 128:(cc + 1) * 128,
                                       u * H4:(u + 1) * H4], after=dep, sem="GD")
                d2 = [w1_ld[i]]
                if i >= 4:
                    assert w1_mm[i - 4] is not None
                    d2.append(w1_mm[i - 4])
                w1_abs_ev[i] = S.emit("vector", lambda v, sl=sl, sa=sa: v.scalar_tensor_tensor(
                    w1abs[:, sa, :], w1chunk[:, sl, :], -1.0,
                    w1chunk[:, sl, :], ALU.mult, ALU.max),
                    after=d2 + gate3("v"), inc="VC")
                d3 = [w1_ld[i]]
                w1_sgn_ev[i] = S.emit("scalar", lambda s, sl=sl, cc=cc, u=u: s.sign(
                    bws1_ap(cc, u * H4, (u + 1) * H4), w1chunk[:, sl, :]),
                    after=d3, inc="SC")
                w1_spill[i] = w1_sgn_ev[i]
                if u == NU - 1:
                    e_b1w[cc] = w1_sgn_ev[i]

            w1_aev = [None] * NW1

            def w1_mmpart(i):
                u = i % NU
                sa = i % 4
                bank = p3_al if i % 2 == 0 else p3_al2
                last = None
                dep0 = [w1_abs_ev[i], e_onesc]
                if i >= 2:
                    dep0.append(w1_aev[i - 2])
                for hl in range(HH4):
                    last = S.emit("tensor",
                                  lambda te, hl=hl, i=i, sa=sa, bank=bank: te.matmul(
                                      bank[:, hl:hl + 1],
                                      w1abs[:, sa, hl * 128:(hl + 1) * 128],
                                      onesc[:, 0:1],
                                      start=(hl == 0), stop=(hl == HH4 - 1)),
                                  after=dep0 if hl == 0 else (),
                                  inc=None if hl < HH4 - 1 else "TC")
                w1_mm[i] = last
                if i // NU == 0:
                    w1_aev[i] = S.emit("vector", lambda v, u=u, bank=bank: v.tensor_scalar(
                        alpha1_pp[:, u * HH4:(u + 1) * HH4], bank[:, 0:HH4],
                        1.0 / C, None, ALU.mult),
                        after=[last], inc="VC")
                else:
                    w1_aev[i] = S.emit("vector", lambda v, u=u, bank=bank: v.scalar_tensor_tensor(
                        alpha1_pp[:, u * HH4:(u + 1) * HH4], bank[:, 0:HH4],
                        1.0 / C, alpha1_pp[:, u * HH4:(u + 1) * HH4],
                        ALU.mult, ALU.add),
                        after=[last, w1_aev[i - NU]], inc="VC")

            e_a2c = S.emit("vector", lambda v: v.tensor_copy(
                a2col[:, :], alpha2_pp[:, :]),
                after=[e_alpha2] + gate3("v"), inc="VC")
            a2t = None
            a2bc_state = {"e_alpha2bc": None}

            def emit_a2bcast():
                a2t = None
                for cc in range(CCH):
                    a2t = S.emit("tensor", lambda te, cc=cc: te.transpose(
                        p3_bcr[0:1, cc * 128:(cc + 1) * 128],
                        a2col[:, cc:cc + 1], iden_sb[:]),
                        after=([e_a2c, e_iden] + gate3("pe")) if cc == 0 else (),
                        inc="TC" if cc == CCH - 1 else None)
                e_a2row = S.emit("vector", lambda v: v.tensor_copy(
                    a2row[0:1, :], p3_bcr[0:1, :]),
                    after=[a2t, e_bo2h[1]], inc="VC")
                e_bc = None
                for h in range(2):
                    e_bcmm = S.emit("tensor", lambda te, h=h: te.matmul(
                        p3_bc[:, :], onesr[0:1, :],
                        a2row[0:1, h * 512:(h + 1) * 512],
                        start=True, stop=True),
                        after=[e_a2row, e_ones, e_bc, e_bo2h[1]], inc="TC")
                    e_bc = S.emit("vector", lambda v, h=h: v.tensor_copy(
                        alpha2_bc[:, h * 512:(h + 1) * 512], p3_bc[:, :]),
                        after=[e_bcmm], inc="VC")
                a2bc_state["e_alpha2bc"] = e_bc


            e_b2m = [None] * HCH

            def S_emit_b2m(hh):
                e_b2w[hh] = S.dma("gpsimd", bws2_sb[:, hh, :],
                                  bws2_s[hh, :, :],
                                  after=[w2_spill[hh]], sem="GD")

            tbank3 = [p3_ta, p3_tb]
            tb3_ev = [None, None]
            ob_ev = [None, None]
            yld = [None] * NT
            xld3 = [None] * NT
            ytr_ev = [None] * NT
            xn_evac = {}
            xn_store = {}
            xn_reuse = {}

            def emit_yld(t):
                dep = [ys_spill[t]] + ph2_snap_c
                if t >= 3:
                    dep.append(ytr_ev[t - 3])
                yld[t] = S.dma("scalar", y_in[:, t % 3, :], y_s[t, :, :],
                               after=dep, sem="SD")

            for t in range(3):
                emit_yld(t)
            for cc in range(CCH):
                e_wo_h[1].append(S.dma(
                    "scalar", wo_sb[:, cc, 512:1024],
                    woT_bf[cc * 128:(cc + 1) * 128, 512:1024],
                    after=[e_wocast] + ph2_snap_c, sem="SD"))
            for t in range(2):
                xld3[t] = S.dma("sync", x_in3[:, t % 2, :],
                                x_h[t * 128:(t + 1) * 128, :], after=[])
            for sp in range(NSP):
                evac_evs = []
                for ttloc in range(4):
                    t = sp * 4 + ttloc
                    ypar = t % 3
                    lastpe, tevs = transpose_8(
                        (lambda ck, ypar=ypar: y_in[:, ypar, ck * 128:(ck + 1) * 128]),
                        yld[t], yT_sb, ttloc * 128, tbank3, tb3_ev)
                    ytr_ev[t] = lastpe
                    evac_evs += tevs
                    if t + 3 < NT:
                        emit_yld(t + 3)
                if sp == 0:
                    emit_bo2()
                for ttloc in range(4):
                    t = sp * 4 + ttloc
                    par = t % 2
                    for h in range(2):
                        bidx = (ttloc * 2 + h) % 2
                        bank = p3_oa if bidx == 0 else p3_ob
                        dep = list(evac_evs) + list(e_wo_h[h]) + \
                            [e_ones, ob_ev[bidx]]
                        for c in range(CCH):
                            S.emit("tensor",
                                   lambda te, bank=bank, c=c, ttloc=ttloc, h=h: te.matmul(
                                       bank[:, :],
                                       yT_sb[:, c, ttloc * 128:(ttloc + 1) * 128],
                                       wo_sb[:, c, h * 512:(h + 1) * 512],
                                       start=(c == 0), stop=False),
                                   after=dep if c == 0 else (), inc=None)
                        last = S.emit("tensor", lambda te, bank=bank, h=h: te.matmul(
                            bank[:, :], onesr[0:1, :],
                            bo2_sb[0:1, h * 512:(h + 1) * 512],
                            start=False, stop=True), after=[e_bo2h[h]],
                            inc="TC")
                        dep4 = [last, xld3[t], xn_reuse.get((t % 2, h))]
                        ev = S.emit("vector",
                                    lambda v, bank=bank, t=t, par=par, h=h: v.tensor_tensor(
                                        xn_sb[:, par, h * 512:(h + 1) * 512],
                                        bank[:, :],
                                        x_in3[:, t % 2, h * 512:(h + 1) * 512],
                                        ALU.add),
                                    after=dep4, inc="VC")
                        ob_ev[bidx] = ev
                        xn_evac[(t, h)] = ev
                    st = S.dma("sync", xn_s[t * 128:(t + 1) * 128, :],
                               xn_sb[:, t % 2, :],
                               after=[xn_evac[(t, 0)], xn_evac[(t, 1)]])
                    xn_reuse[(t % 2, 0)] = st
                    xn_reuse[(t % 2, 1)] = st
                    xn_store[t] = st
                    if t + 2 < NT:
                        xld3[t + 2] = S.dma(
                            "sync", x_in3[:, t % 2, :],
                            x_h[(t + 2) * 128:(t + 3) * 128, :],
                            after=[xn_evac[(t, 0)], xn_evac[(t, 1)]])
                for i in range(4 * sp, 4 * sp + 4):
                    if i < NW1:
                        if i - 4 >= 0:
                            w1_mmpart(i - 4)
                        w1_chain(i)
                for i in range(4 * (sp - 2), 4 * (sp - 2) + 4):
                    if 0 <= i < HCH:
                        S_emit_b2m(i)
                if sp == 0:
                    emit_a2bcast()
            for i in range(min(4 * NSP, NW1), NW1):
                if i - 4 >= 0:
                    w1_mmpart(i - 4)
                w1_chain(i)
            for i in range(max(NW1 - 4, 0), NW1):
                w1_mmpart(i)
            for i in range(max(4 * (NSP - 2), 0), HCH):
                S_emit_b2m(i)
            e_alpha1 = w1_aev[NW1 - 1]
            e_alpha1 = S.emit("vector", lambda v: v.engine_nop(),
                              after=w1_aev, inc="VC")
            if debug:
                S.dma("sync", dbg_a1[:, :], alpha1_pp[:, :], after=[e_alpha1])
                S.dma("gpsimd", dbg_a2[:, :], alpha2_bc[:, :],
                      after=[a2bc_state["e_alpha2bc"]], sem="GD")
                for cc in range(CCH):
                    S.dma("gpsimd", bws1_s[cc, :, :], bws1_ap(cc, 0, H),
                          after=[e_b1w[cc], e_alpha1], sem="GD")
            ph3_snap_c = S.snap()
            ph3_snap = ph3_snap_c + S.snap_dma()
        st3.close()

        st4 = ExitStack()
        if True:
            xn_in = A(st4, "xn_in", [128, 4, C], F32)
            xn_res = A(st4, "xn_res", [128, 2, C], F32)
            ln2_rg = A(st4, "ln2_rg", [128, 4, C], BF16)
            ln2T_sb = A(st4, "ln2T_sb", [128, CCH, 512], BF16)
            hT_sb = A(st4, "hT_sb", [128, HCH, 512], BF16)
            st4_sum = A(st4, "st4_sum", [128, 4], F32)
            st4_m2 = A(st4, "st4_m2", [128, 4], F32)
            st4_ssq = A(st4, "st4_ssq", [128, 4], F32)
            st4_ngm = A(st4, "st4_ngm", [128, 4], F32)
            st4_m2e = A(st4, "st4_m2e", [128, 4], F32)
            st4_rsd = A(st4, "st4_rsd", [128, 4], F32)
            p4_ha = P(st4, "p4_ha", [128, 512], F32)
            p4_hb = P(st4, "p4_hb", [128, 512], F32)
            p4_oa = P(st4, "p4_oa", [128, 512], F32)
            p4_ob = P(st4, "p4_ob", [128, 512], F32)
            p4_ta = P(st4, "p4_ta", [128, 4, 128], BF16)
            p4_tb = P(st4, "p4_tb", [128, 4, 128], BF16)
            first4 = {"pe": ph3_snap_c, "v": ph3_snap, "s": ph3_snap,
                      "sync": ph3_snap, "gp": ph3_snap}

            def gate4(key):
                return first4[key]


            tbank4 = [p4_ta, p4_tb]
            tb4_ev = [None, None]
            hbank = [p4_ha, p4_hb]
            hb_ev = [None, None]
            obank = [p4_oa, p4_ob]
            ob4_ev = [None, None]
            xnld = [None] * NT
            out_st = [None] * NT
            xrld = [None] * NT
            ln2v = [None] * NT
            sq4 = [None] * NT
            lt4 = [None] * NT
            outev = {}
            ffn2_done = {}
            stats4 = (st4_sum, st4_ssq, st4_ngm, st4_m2, st4_m2e, st4_rsd)

            span_tevs_by_blk = {}

            def emit_ln2_chains(blk):
                sums = {}
                b1 = {}
                stds = {}
                for ttloc in range(4):
                    t = blk * 4 + ttloc
                    dep = [xn_store[t]] + gate4("sync")
                    if t >= 4:
                        dep += [ln2v[t - 4], sq4[t - 4]]
                    xnld[t] = S.dma("sync", xn_in[:, t % 4, :],
                                    xn_s[t * 128:(t + 1) * 128, :], after=dep)
                    sums[t] = ln_tile_a(t, xn_in[:, t % 4, :], xnld[t], ln2_rg,
                                        stats4, lt4[t - 4] if t >= 4 else None,
                                        sq4)
                for ttloc in range(4):
                    t = blk * 4 + ttloc
                    b1[t] = ln_tile_b1(t, stats4, sums[t], sq4)
                for ttloc in range(4):
                    t = blk * 4 + ttloc
                    stds[t] = ln_tile_bstd(t, stats4, b1[t][1])
                for ttloc in range(4):
                    t = blk * 4 + ttloc
                    ln_tile_b2(t, xn_in[:, t % 4, :], xnld[t], ln2_rg, stats4,
                               b1[t][0], stds[t], ln2v, sq4)

            def emit_ln2_transposes(blk):
                span_tevs = []
                for ttloc in range(4):
                    t = blk * 4 + ttloc
                    extra = gate4("pe") if t == 0 else []
                    lastpe, tevs = transpose_8(
                        (lambda ck, t=t: ln2_rg[:, t % 4, ck * 128:(ck + 1) * 128]),
                        ln2v[t], ln2T_sb, ttloc * 128, tbank4, tb4_ev,
                        extra=extra, evac_eng="scalar")
                    lt4[t] = lastpe
                    span_tevs += tevs
                span_tevs_by_blk[blk] = span_tevs

            emit_ln2_chains(0)
            if debug:
                S.dma("gpsimd", dbg_b2sb[:, :, :], bws2_sb[:, :, :],
                      after=e_b2m, sem="GD")
                S.dma("gpsimd", dbg_a2p4[:, :], alpha2_bc[:, :],
                      after=e_b2m, sem="GD")
            emit_ln2_transposes(0)
            if NSP > 1:
                emit_ln2_chains(1)
            for hh in range(HCH):
                e_b2m[hh] = S.emit("vector", lambda v, hh=hh: v.tensor_tensor(
                    bws2_sb[:, hh, :], bws2_sb[:, hh, :], alpha2_bc[:, :],
                    ALU.mult),
                    after=[e_b2w[hh], e_b2r, a2bc_state["e_alpha2bc"]],
                    inc="VC")
            for blk in range(NSP):
                span_tevs = span_tevs_by_blk[blk]
                ffn1_evs = []
                for hh in range(HCH):
                    bank = hbank[hh % 2]
                    dep = list(span_tevs) + list(e_b1w) + [hb_ev[hh % 2]]
                    if blk == 0:
                        dep.append(e_b1w[CCH - 1])
                    last = None
                    for c in range(CCH):
                        last = S.emit("tensor",
                                      lambda te, bank=bank, c=c, hh=hh: te.matmul(
                                          bank[:, :],
                                          bws1_ap(c, hh * 128, (hh + 1) * 128),
                                          ln2T_sb[:, c, :],
                                          start=(c == 0), stop=(c == CCH - 1)),
                                      after=dep if c == 0 else (),
                                      inc=None if c < CCH - 1 else "TC")
                    ev = S.emit("scalar", lambda s, bank=bank, hh=hh: s.activation(
                        hT_sb[:, hh, :], bank[:, :], AF.Gelu,
                        bias=b1p_sb[:, hh:hh + 1], scale=alpha1_pp[:, hh:hh + 1]),
                        after=[last, e_alpha1, e_b1p] + (gate4("s") if hh == 0 else []),
                        inc="SC")
                    hb_ev[hh % 2] = ev
                    ffn1_evs.append(ev)

                if debug and blk == 0:
                    S.dma("gpsimd", dbg_hT[:, :, :], hT_sb[:, :, :],
                          after=ffn1_evs, sem="GD")
                if blk + 1 < NSP:
                    emit_ln2_transposes(blk + 1)
                if blk + 2 < NSP:
                    emit_ln2_chains(blk + 2)
                for ttloc in range(4):
                    t = blk * 4 + ttloc
                    par = t % 2
                    dep0 = [xn_store[t], out_st[t - 2] if t >= 2 else None] \
                        + gate4("gp")
                    xrld[t] = S.dma("gpsimd", xn_res[:, par, :],
                                    xn_s[t * 128:(t + 1) * 128, :], after=dep0,
                                    sem="GD")
                    for h in range(2):
                        bidx = (ttloc * 2 + h) % 2
                        bank = obank[bidx]
                        dep = list(ffn1_evs) + e_b2m + \
                            [e_b2r, e_ones, ob4_ev[bidx]]
                        for hh in range(HCH):
                            S.emit("tensor",
                                   lambda te, bank=bank, hh=hh, ttloc=ttloc, h=h: te.matmul(
                                       bank[:, :],
                                       hT_sb[:, hh, ttloc * 128:(ttloc + 1) * 128],
                                       bws2_sb[:, hh, h * 512:(h + 1) * 512],
                                       start=(hh == 0), stop=False),
                                   after=dep if hh == 0 else (), inc=None)
                        last = S.emit("tensor", lambda te, bank=bank, h=h: te.matmul(
                            bank[:, :], onesr[0:1, :],
                            b2r_sb[0:1, h * 512:(h + 1) * 512],
                            start=False, stop=True), inc="TC")
                        dep4 = [last, xrld[t], outev.get((t - 2, h))]
                        ev = S.emit("vector",
                                    lambda v, bank=bank, par=par, h=h: v.tensor_tensor(
                                        xn_res[:, par, h * 512:(h + 1) * 512],
                                        bank[:, :],
                                        xn_res[:, par, h * 512:(h + 1) * 512],
                                        ALU.add),
                                    after=dep4, inc="VC")
                        ob4_ev[bidx] = ev
                        outev[(t, h)] = ev
                    out_st[t] = S.dma("gpsimd", out_h[t * 128:(t + 1) * 128, :],
                                      xn_res[:, t % 2, :],
                                      after=[outev[(t, 0)], outev[(t, 1)]],
                                      sem="OS")
                ffn2_done[blk] = outev[(blk * 4 + 3, 1)]

            allq = []
            for i in range(S.NSEM):
                for base in ("LD", "GD", "OS", "SD"):
                    nm = f"{base}{i}"
                    if S.planned.get(nm, 0):
                        allq.append((nm, S.planned[nm]))
            S.emit("sync", lambda sy: sy.nop(), after=allq)
        st4.close()
        st34.close()

        S.check_deadlock()
        sem_stack = ExitStack()
        sems = {}
        for name in (["TC", "VC", "SC", "GC"] +
                     [f"LD{i}" for i in range(S.NSEM)] +
                     [f"GD{i}" for i in range(S.NSEM)] +
                     [f"OS{i}" for i in range(S.NSEM)] +
                     [f"SD{i}" for i in range(S.NSEM)]):
            sems[name] = sem_stack.enter_context(nc.semaphore("s" + name))
        with nc.Block() as block:
            S.replay(block, sems)
        sem_stack.close()
    stp.close()
    return nc


def make_base_inputs(Wq, bq, Wk, bk, Wv, bv, Wo, bo, w1, b1, w2, b2):
    f = lambda a: np.ascontiguousarray(np.asarray(a, dtype=np.float32))
    d = {
        "wqT": f(np.asarray(Wq).T), "wkT": f(np.asarray(Wk).T),
        "wvT": f(np.asarray(Wv).T), "woT": f(np.asarray(Wo).T),
        "w1T": f(np.asarray(w1).T), "w2T": f(np.asarray(w2).T),
        "bqp": f(np.asarray(bq).reshape(CCH, 128).T),
        "bkp": f(np.asarray(bk).reshape(CCH, 128).T),
        "b1p": f(np.asarray(b1).reshape(HCH, 128).T),
        "bvp": f(np.asarray(bv).reshape(CCH, 128).T),
        "bor": f(np.asarray(bo).reshape(1, C)),
        "b2r": f(np.asarray(b2).reshape(1, C)),
        "iden": np.eye(128, dtype=ml_dtypes.bfloat16),
    }
    a = np.arange(128)
    sub = np.where(a[None, :] >= a[:, None] + 128 - WINDOW, 0.0, NEG_INF)
    dia = np.where(a[None, :] >= a[:, None] - WINDOW, 0.0, NEG_INF)
    d["maskst"] = np.concatenate([sub, dia], axis=1).astype(np.float32)
    return d


def kernel(L0_attn_x, L0_attn_Wq, L0_attn_bq, L0_attn_Wk, L0_attn_bk,
           L0_attn_Wv, L0_attn_bv, L0_attn_Wo, L0_attn_bo,
           L0_ffn_w1, L0_ffn_b1, L0_ffn_w2, L0_ffn_b2):
    x = np.asarray(L0_attn_x)
    B, T, _ = x.shape
    nc = build_program(T)
    base = make_base_inputs(L0_attn_Wq, L0_attn_bq, L0_attn_Wk, L0_attn_bk,
                            L0_attn_Wv, L0_attn_bv, L0_attn_Wo, L0_attn_bo,
                            L0_ffn_w1, L0_ffn_b1, L0_ffn_w2, L0_ffn_b2)
    in_maps = []
    for b in range(B):
        m = dict(base)
        m["x"] = np.ascontiguousarray(x[b].astype(np.float32, copy=False))
        in_maps.append(m)
    res = run_bass_kernel_spmd(nc, in_maps, core_ids=list(range(B)))
    return np.stack([r["out"] for r in res.results]).astype(np.float32)
